# version 20
# baseline (speedup 1.0000x reference)
import numpy as np
from contextlib import ExitStack
import concourse.bass as bass
import concourse.mybir as mybir
from concourse.bass_utils import run_bass_kernel_spmd

F32 = mybir.dt.float32
BF16 = mybir.dt.bfloat16
ALU = mybir.AluOpType
AF = mybir.ActivationFunctionType

P = 128
D = 2048
HID = 8192
TCORE = 1024
TB = 512
NCH = TCORE // TB
NT = D // P
NK = D // P
NDH = HID // P
NG = 4
INV_D = 1.0 / D
EPS = 1e-5
SC_H = float(1.0 / np.sqrt(HID))

ENGINES = ("sync", "tensor", "vector", "scalar", "gpsimd")
DMA_SEMS = {"sC", "sWH0", "sWH1", "sWH2", "sWL0", "sWL1", "sWL2",
            "sCV0", "sCV1", "sCVH0", "sCVH1", "sCVL0", "sCVL1",
            "sL", "sXR0", "sXR1", "sSP", "sRL0", "sRL1",
            "sF1", "sFS", "sFR0", "sFR1", "sX2Y", "sO"}
SEM_NAMES = tuple(sorted(DMA_SEMS)) + ("s_t", "s_v", "s_a", "s_g")


class Em:

    def __init__(self):
        self.ops = {e: [] for e in ENGINES}
        self.cnt = {s: 0 for s in SEM_NAMES}
        self.own = {e: {s: 0 for s in SEM_NAMES} for e in ENGINES}
        self.waited = {e: {s: 0 for s in SEM_NAMES} for e in ENGINES}

    def emit(self, eng, fn, waits=(), inc=None):
        best = {}
        for s, v in waits:
            if v > best.get(s, 0):
                best[s] = v
        real = []
        for s, v in best.items():
            if self.waited[eng][s] >= v:
                continue
            real.append((s, v))
            self.waited[eng][s] = v
        tok = None
        if inc is not None:
            s, amt = inc
            self.cnt[s] += amt
            self.own[eng][s] += amt
            tok = (s, self.cnt[s])
        self.ops[eng].append((tuple(real), fn, inc))
        return tok

    def cur(self, sem):
        return self.cnt[sem]

    def fence(self):
        return [(s, self.cnt[s]) for s in SEM_NAMES]


class Scr:

    def __init__(self, tshi, tslo, tshi2, tslo2, rowf, sqq):
        self.tshi, self.tslo = tshi, tslo
        self.tshi2, self.tslo2 = tshi2, tslo2
        self.rowf = rowf
        self.sqq = sqq

R_M, R_S, R_NM, R_T0 = range(4)


def build_nc():
    nc = bass.Bass("TRN2", target_bir_lowering=False, debug=False)

    def din(name, shape, dt=F32):
        return nc.dram_tensor(name, shape, dt, kind="ExternalInput").ap()

    x_fm = din("x_fm", [D, TCORE])
    xc_fm = din("xc_fm", [D, TCORE])
    wi = din("wi", [D, D])
    wf = din("wf", [D, D])
    wg = din("wg", [D, D])
    wo = din("wo", [D, D])
    w1 = din("w1", [HID, D])
    w3 = din("w3", [HID, D])
    w2 = din("w2", [NG * NDH * P, TB])
    gnw = din("gnw", [P, NT])
    h128 = din("h128", [P, P], BF16)
    onescol = din("onescol", [P, 1], BF16)
    onesrow = din("onesrow", [1, P])
    maskcol = din("maskcol", [P, 1])
    wpair_dram = {}
    for nm, ap_, rows in (("wi", wi, D), ("wf", wf, D), ("wg", wg, D),
                          ("wo", wo, D), ("w1", w1, HID), ("w3", w3, HID),
                          ("w2", w2, NG * NDH * P)):
        cols = TB if nm == "w2" else D
        hi_t = nc.dram_tensor(nm + "_h", [rows, cols], BF16, kind="Internal").ap()
        lo_t = nc.dram_tensor(nm + "_l", [rows, cols], BF16, kind="Internal").ap()
        wpair_dram[nm] = (ap_, hi_t, lo_t, rows, cols)
    y_fm = nc.dram_tensor("y_fm", [D, TCORE], F32, kind="ExternalOutput").ap()
    x2s = nc.dram_tensor("x2s", [D, TCORE], F32, kind="Internal").ap()
    s3row = nc.dram_tensor("s3row", [1, TCORE], F32, kind="Internal").ap()
    nm3row = nc.dram_tensor("nm3row", [1, TCORE], F32, kind="Internal").ap()
    fs = nc.dram_tensor("fs", [D, TB], F32, kind="Internal").ap()

    em = Em()

    with ExitStack() as ctx:
        def sb(name, shape, dt=F32):
            return ctx.enter_context(nc.sbuf_tensor(name, shape, dt))

        def psum(name, shape):
            return ctx.enter_context(nc.psum_tensor(name, shape, F32))

        c_h128 = sb("c_h128", [P, P], BF16)
        c_onescol = sb("c_onescol", [P, 1], BF16)
        c_onesrow = sb("c_onesrow", [1, P])
        c_maskcol = sb("c_maskcol", [P, 1])
        c_gnw = sb("c_gnw", [P, NT])
        cstate = sb("cstate", [P, NT])
        cstate_c = sb("cstate_c", [P, NT])
        whi = [sb(f"whi{i}", [P, 2048], BF16) for i in range(2)] + [None]
        wlo = [sb(f"wlo{i}", [P, 2048], BF16) for i in range(2)] + [None]
        wnslots = [2]
        sqs = sb("sqs", [P, TB])
        sgs = sb("sgs", [P, TB])
        sbc = sb("sbc", [P, TB])
        nmbc = sb("nmbc", [P, TB])
        g_rows = sb("g_rows", [1, 4 * TB])

        def g_rowf(i):
            return g_rows[0:1, i * TB:(i + 1) * TB]

        pA = psum("pA", [P, TB])
        pB = psum("pB", [P, TB])
        pC = psum("pC", [P, TB])
        pD = psum("pD", [P, TB])
        pE = psum("pE", [P, TB])
        pF = psum("pF", [P, TB])
        psum_free = {id(p): [] for p in (pA, pB, pC, pD, pE, pF)}

        def dma(eng, sem, out_ap, in_ap, waits=()):
            selfw = [(sem, em.cur(sem))]
            return em.emit(eng, lambda e, o=out_ap, i=in_ap: e.dma_start(out=o, in_=i),
                           waits=list(waits) + selfw, inc=(sem, 16))

        def vop(fn, waits=()):
            return em.emit("vector", fn, waits=waits, inc=("s_v", 1))

        def aop(fn, waits=()):
            return em.emit("scalar", fn, waits=waits, inc=("s_a", 1))

        def gop(fn, waits=()):
            return em.emit("gpsimd", fn, waits=waits, inc=("s_g", 1))

        def top(fn, waits=(), inc=True):
            return em.emit("tensor", fn, waits=waits,
                           inc=(("s_t", 1) if inc else None))

        for dst, srcap in ((c_h128, h128), (c_onescol, onescol),
                           (c_onesrow, onesrow), (c_maskcol, maskcol),
                           (c_gnw, gnw)):
            dma("gpsimd", "sC", dst[:], srcap[:])
        t_memset = gop(lambda e: e.memset(cstate_c[:], 0.0))
        statics_w = [("sC", em.cur("sC")), t_memset]

        cw_holder, cwh_holder, cwl_holder = [], [], []

        class _H:
            def __init__(self, h, i):
                self.h = h
                self.i = i
            def __getitem__(self, k):
                return self.h[self.i][k]

        cw = [_H(cw_holder, 0), _H(cw_holder, 1)]
        cwh = [_H(cwh_holder, 0), _H(cwh_holder, 1)]
        cwl = [_H(cwl_holder, 0), _H(cwl_holder, 1)]

        conv_order = ("wi", "wf", "wg", "wo", "w1", "w3", "w2")
        conv_queue = []
        for nm in conv_order:
            _, _, _, rows, cols = wpair_dram[nm]
            for si in range(rows // P):
                for col0 in range(0, cols, 1024):
                    conv_queue.append((nm, si, col0, min(1024, cols - col0)))
        conv_tok = {}
        conv_part = {}
        conv_next = [0]
        conv_busy = [{"cw": [], "cwh": [], "cwl": []} for _ in range(2)]

        conv_in_tok = {}
        conv_in_next = [0]

        def _conv_emit_in(k):
            if k >= len(conv_queue) or k in conv_in_tok:
                return
            nm, si, col0, nc_ = conv_queue[k]
            p = k % 2
            src_w, _, _, rows, cols = wpair_dram[nm]
            rs = slice(si * P, (si + 1) * P)
            cs = slice(col0, col0 + nc_)
            conv_in_tok[k] = dma("gpsimd", "sCV0" if p == 0 else "sCV1",
                                 cw[p][:, 0:nc_], src_w[rs, cs],
                                 waits=conv_busy[p]["cw"])

        def conv_pump(n=1):
            for _ in range(n):
                k = conv_next[0]
                if k >= len(conv_queue):
                    return
                conv_next[0] += 1
                _conv_emit_in(k)
                _conv_emit_in(k + 1)
                nm, si, col0, nc_ = conv_queue[k]
                p = k % 2
                _, hi_t, lo_t, rows, cols = wpair_dram[nm]
                rs = slice(si * P, (si + 1) * P)
                cs = slice(col0, col0 + nc_)
                bz = conv_busy[p]
                t_in = conv_in_tok.pop(k)
                t_hi = aop(lambda e, p=p, c=nc_: e.copy(out=cwh[p][:, 0:c],
                           in_=cw[p][:, 0:c]),
                           waits=[t_in] + bz["cwh"])
                t_lo = gop(lambda e, p=p, c=nc_: e.tensor_sub(
                    cwl[p][:, 0:c], cw[p][:, 0:c], cwh[p][:, 0:c]),
                    waits=[t_in, t_hi] + bz["cwl"])
                t_oh = dma("gpsimd", "sCVH0" if p == 0 else "sCVH1",
                           hi_t[rs, cs], cwh[p][:, 0:nc_], waits=[t_hi])
                t_ol = dma("gpsimd", "sCVL0" if p == 0 else "sCVL1",
                           lo_t[rs, cs], cwl[p][:, 0:nc_], waits=[t_lo])
                bz["cw"] = [t_hi, t_lo]
                bz["cwh"] = [t_oh]
                bz["cwl"] = [t_ol]
                key = (nm, si)
                conv_part.setdefault(key, []).extend([t_oh, t_ol])
                if col0 + nc_ >= cols:
                    conv_tok[key] = conv_part.pop(key)

        def conv_require(nm, si):
            while (nm, si) not in conv_tok:
                conv_pump(1)
            return conv_tok[(nm, si)]

        wslot_busy = [[], [], []]
        wcounter = [0]

        def wstream(nm, si):
            _, hi_t, lo_t, rows, cols = wpair_dram[nm]
            cv = conv_require(nm, si)
            i = wcounter[0]
            wcounter[0] += 1
            slot = i % wnslots[0]
            rs = slice(si * P, (si + 1) * P)
            t_h = dma("sync", ("sWH0", "sWH1", "sWH2")[slot],
                      whi[slot][:, 0:cols], hi_t[rs, :],
                      waits=cv + wslot_busy[slot])
            t_l = dma("sync", ("sWL0", "sWL1", "sWL2")[slot],
                      wlo[slot][:, 0:cols], lo_t[rs, :],
                      waits=cv + wslot_busy[slot])

            def done_cb(mm_tok, s=slot):
                wslot_busy[s] = [mm_tok]
            return slot, [t_h, t_l], done_cb

        def mm3_group(bank, nk, lhsT_hi_f, lhsT_lo_f, rhs_hi_f, rhs_lo_f, waits,
                      rhs_waits_f=None):
            pf = psum_free[id(bank)]
            tok = None
            for k in range(nk):
                w0 = (list(waits) + pf) if k == 0 else []
                if rhs_waits_f is not None:
                    w0 = w0 + list(rhs_waits_f(k))
                first = (k == 0)
                last = (k == nk - 1)
                top(lambda e, a=lhsT_hi_f(k), b=rhs_hi_f(k), s=first: e.matmul(
                    bank[:], a, b, start=s, stop=False), waits=w0, inc=False)
                top(lambda e, a=lhsT_hi_f(k), b=rhs_lo_f(k): e.matmul(
                    bank[:], a, b, start=False, stop=False), inc=False)
                tok = top(lambda e, a=lhsT_lo_f(k), b=rhs_hi_f(k), s=last: e.matmul(
                    bank[:], a, b, start=False, stop=s))
            return tok

        def psum_release(bank, tok):
            psum_free[id(bank)] = [tok]

        stats_busy = [[]]
        rows_busy = [[]]

        class StatsAcc:
            def __init__(self, scr, ntiles, want_sum=True, single=False):
                self.scr = scr
                self.n = ntiles
                self.want_sum = want_sum
                self.single = single
                self.j = 0
                self.prev_sum = [list(stats_busy[0]), list(stats_busy[0])]
                self.prev_sq = [list(stats_busy[0]), list(stats_busy[0])]
                self.tok_sum = None
                self.tok_sq = None

            def tile(self, t_ap, waits):
                scr = self.scr
                j, n = self.j, self.n
                self.j += 1
                p = 0 if self.single else j % 2
                jmin = 1 if self.single else 2
                if self.want_sum:
                    ps = self.prev_sum[p] if j >= jmin else list(stats_busy[0])
                    t_hi = aop(lambda e, t=t_ap, p=p: e.copy(out=scr.tshi[p][:],
                               in_=t), waits=list(waits) + ps)
                    t_lo = vop(lambda e, t=t_ap, p=p: e.tensor_sub(scr.tslo[p][:],
                               t, scr.tshi[p][:]),
                               waits=list(waits) + [t_hi] + ps)
                    w0 = [t_hi, t_lo] + (psum_free[id(pE)] if j == 0 else [])
                    top(lambda e, s=(j == 0), p=p: e.matmul(pE[0:1, :],
                        c_onescol[:], scr.tshi[p][:], start=s, stop=False),
                        waits=w0, inc=False)
                    self.tok_sum = top(lambda e, s=(j == n - 1), p=p: e.matmul(
                        pE[0:1, :], c_onescol[:], scr.tslo[p][:], start=False,
                        stop=s))
                    self.prev_sum[p] = [self.tok_sum]
                pq = self.prev_sq[p] if j >= jmin else list(stats_busy[0])
                t_sq = vop(lambda e, t=t_ap, p=p: e.tensor_mul(scr.sqq[p][:], t, t),
                           waits=list(waits) + pq)
                t_hi2 = aop(lambda e, p=p: e.copy(out=scr.tshi2[p][:], in_=scr.sqq[p][:]),
                            waits=[t_sq] + pq)
                t_lo2 = vop(lambda e, p=p: e.tensor_sub(scr.tslo2[p][:], scr.sqq[p][:],
                            scr.tshi2[p][:]),
                            waits=[t_sq, t_hi2] + pq)
                w0 = [t_hi2, t_lo2] + (psum_free[id(pF)] if j == 0 else [])
                top(lambda e, s=(j == 0), p=p: e.matmul(pF[0:1, :], c_onescol[:],
                    scr.tshi2[p][:], start=s, stop=False), waits=w0, inc=False)
                self.tok_sq = top(lambda e, s=(j == n - 1), p=p: e.matmul(
                    pF[0:1, :], c_onescol[:], scr.tslo2[p][:], start=False, stop=s))
                self.prev_sq[p] = [self.tok_sq]
                return ([self.tok_sum] if self.want_sum else []) + [self.tok_sq]

            def finish(self):
                stats_busy[0] = (([self.tok_sum] if self.tok_sum else [])
                                 + [self.tok_sq])
                return self.tok_sum, self.tok_sq

        def ln_rows(scr, tok_sum, tok_sq, nfeat=D):
            rb = rows_busy[0]
            r = g_rowf
            inv = 1.0 / nfeat
            t_m = aop(lambda e: e.copy(out=r(R_M), in_=pE[0:1, :]),
                      waits=[tok_sum] + rb)
            t_m2 = aop(lambda e: e.mul(out=r(R_M), in_=r(R_M), mul=inv),
                       waits=[t_m])
            t_q = aop(lambda e: e.copy(out=r(R_T0), in_=pF[0:1, :]),
                      waits=[tok_sq] + rb)
            t_msq = vop(lambda e: e.tensor_mul(r(R_NM), r(R_M), r(R_M)),
                        waits=[t_m2] + rb)
            t_v = vop(lambda e: e.scalar_tensor_tensor(r(R_T0), r(R_T0), inv,
                      r(R_NM), ALU.mult, ALU.subtract), waits=[t_q, t_msq] + rb)
            t_ve = vop(lambda e: e.tensor_scalar_add(r(R_T0), r(R_T0), EPS),
                       waits=[t_v])
            t_rc = vop(lambda e: e.reciprocal(r(R_NM), r(R_T0)),
                       waits=[t_ve, t_msq] + rb)
            t_s = aop(lambda e: e.activation(out=r(R_S), in_=r(R_NM),
                      func=AF.Sqrt), waits=[t_rc] + rb)
            t_nm = vop(lambda e: e.scalar_tensor_tensor(r(R_NM), r(R_M), -1.0,
                       r(R_S), ALU.mult, ALU.mult), waits=[t_m2, t_s, t_rc] + rb)
            psum_release(pE, t_m)
            psum_release(pF, t_q)
            rows_busy[0] = [t_s, t_nm]
            return t_s, t_nm

        def bcast(row_ap, dst, waits):
            pf = psum_free[id(pD)]
            t_mm = top(lambda e, r=row_ap: e.matmul(pD[:], c_onesrow[:], r,
                                                    start=True, stop=True),
                       waits=list(waits) + pf + statics_w)
            t_ev = vop(lambda e, d=dst: e.tensor_copy(d[:], pD[:]), waits=[t_mm])
            psum_release(pD, t_ev)
            rows_busy[0] = rows_busy[0] + [t_mm]
            return t_ev

        def tcol(buf, j, width=TB):
            return buf[:, j * width:(j + 1) * width]

        with ExitStack() as attn_ctx:
            def asb(name, shape, dt=F32):
                return attn_ctx.enter_context(nc.sbuf_tensor(name, shape, dt))
            xbuf = asb("xbuf", [P, NT * TB])
            vbuf = asb("vbuf", [P, NT * TB])
            lnhi = asb("lnhi", [P, NT * TB], BF16)
            lnlo = asb("lnlo", [P, NT * TB], BF16)
            oghi = asb("oghi", [P, NT * TB], BF16)
            oglo = asb("oglo", [P, NT * TB], BF16)
            xr0 = asb("xr0", [P, TB])
            xr1 = asb("xr1", [P, TB])
            x2t = asb("x2t", [P, TB])
            a_tshi = [asb(f"a_tshi{i}", [P, TB], BF16) for i in range(2)]
            a_tslo = [asb(f"a_tslo{i}", [P, TB], BF16) for i in range(2)]
            a_tshi2 = [asb(f"a_tshi2{i}", [P, TB], BF16) for i in range(2)]
            a_tslo2 = [asb(f"a_tslo2{i}", [P, TB], BF16) for i in range(2)]
            a_sqq = [asb(f"a_sqq{i}", [P, TB]) for i in range(2)]
            for i in range(2):
                cw_holder.append(asb(f"cw{i}", [P, 1024]))
                cwh_holder.append(asb(f"cwh{i}", [P, 1024], BF16))
                cwl_holder.append(asb(f"cwl{i}", [P, 1024], BF16))
            conv_pump(16)
            scrA = Scr(a_tshi, a_tslo, a_tshi2, a_tslo2, g_rowf, a_sqq)

            def load_and_ln1(src_dram, mc, hi_buf=None, lo_buf=None):
                if hi_buf is None:
                    hi_buf, lo_buf = lnhi, lnlo
                fw = em.fence()
                src_chunk = src_dram[:, mc * TB:(mc + 1) * TB].rearrange(
                    "(j p) c -> p j c", p=P)
                dst_chunk = xbuf[:, 0:NT * TB].rearrange("p (j c) -> p j c", j=NT)
                t_ld = dma("gpsimd", "sL", dst_chunk, src_chunk, waits=fw)
                ld = [t_ld] * NT
                acc = StatsAcc(scrA, NT)
                for j in range(NT):
                    acc.tile(tcol(xbuf, j), [ld[j]])
                tok_sum, tok_sq = acc.finish()
                t_s, t_nm = ln_rows(scrA, tok_sum, tok_sq)
                t_sbc = bcast(g_rowf(R_S), sbc, [t_s])
                t_nmbc = bcast(g_rowf(R_NM), nmbc, [t_nm])
                pair_ready = []
                prev_t3 = []
                prev_t2 = []
                prev_t4 = []
                for j in range(NT):
                    t1 = vop(lambda e, j=j: e.tensor_mul(sqs[:], tcol(xbuf, j),
                                                         sbc[:]),
                             waits=[ld[j], t_nmbc, t_sbc] + prev_t2)
                    t2 = vop(lambda e: e.tensor_add(sgs[:], sqs[:], nmbc[:]),
                             waits=[t1] + prev_t3 + prev_t4)
                    t3 = aop(lambda e, j=j: e.copy(out=tcol(hi_buf, j), in_=sgs[:]),
                             waits=[t2])
                    t4 = gop(lambda e, j=j: e.tensor_sub(tcol(lo_buf, j), sgs[:],
                                                         tcol(hi_buf, j)),
                             waits=[t2, t3])
                    prev_t3 = [t3]
                    prev_t2 = [t2]
                    prev_t4 = [t4]
                    pair_ready.append([t3, t4])
                return pair_ready

            def proj_weight(wsrc, dt, pair_ready, bank):
                conv_pump(4)
                slot, ready, done_cb = wstream(wsrc, dt)
                tok = mm3_group(
                    bank, NK,
                    lambda k, s=slot: whi[s][:, k * P:(k + 1) * P],
                    lambda k, s=slot: wlo[s][:, k * P:(k + 1) * P],
                    lambda k: tcol(lnhi, k),
                    lambda k: tcol(lnlo, k),
                    ready,
                    rhs_waits_f=lambda k: pair_ready[k])
                done_cb(tok)
                return tok

            carry_tok_c = [[] for _ in range(NT)]
            carry_tok = [[] for _ in range(NT)]
            pr0 = load_and_ln1(xc_fm, 1)
            sc_prev = [t_memset]
            vb_busy = [[] for _ in range(4)]
            for dt in range(NT):
                conv_pump(6)
                slot, ready, done_i = wstream("wi", dt)
                tok_i0 = mm3_group(
                    (pA, pB)[dt % 2], NK,
                    lambda k, s=slot: whi[s][:, k * P:(k + 1) * P],
                    lambda k, s=slot: wlo[s][:, k * P:(k + 1) * P],
                    lambda k: tcol(lnhi, k), lambda k: tcol(lnlo, k),
                    ready, rhs_waits_f=lambda k: pr0[k])
                done_i(tok_i0)
                bk_i = (pA, pB)[dt % 2]
                t_si0 = aop(lambda e, b=bk_i: e.activation(out=tcol(vbuf, 0),
                            in_=b[:], func=AF.Silu), waits=[tok_i0] + vb_busy[0])
                psum_release(bk_i, t_si0)
                slot, ready, done_f = wstream("wf", dt)
                tok_f0 = mm3_group(
                    (pC, pD)[dt % 2], NK,
                    lambda k, s=slot: whi[s][:, k * P:(k + 1) * P],
                    lambda k, s=slot: wlo[s][:, k * P:(k + 1) * P],
                    lambda k: tcol(lnhi, k), lambda k: tcol(lnlo, k),
                    ready, rhs_waits_f=lambda k: pr0[k])
                done_f(tok_f0)
                bk_f = (pC, pD)[dt % 2]
                t_gt0 = aop(lambda e, b=bk_f: e.activation(out=tcol(vbuf, 2),
                            in_=b[:], func=AF.Sigmoid), waits=[tok_f0] + vb_busy[2])
                psum_release(bk_f, t_gt0)
                t_v1 = vop(lambda e: e.tensor_mul(sqs[:], tcol(vbuf, 0),
                           tcol(vbuf, 2)), waits=[t_si0, t_gt0])
                t_v2 = vop(lambda e: e.tensor_sub(tcol(vbuf, 0), tcol(vbuf, 0),
                           sqs[:]), waits=[t_v1])
                t_sc0 = vop(lambda e, dt=dt: e.tensor_tensor_scan(
                    tcol(vbuf, 0), tcol(vbuf, 2), tcol(vbuf, 0),
                    cstate_c[:, dt:dt + 1], ALU.mult, ALU.add),
                    waits=[t_v2, t_memset] + sc_prev)
                t_cs0 = gop(lambda e, dt=dt: e.tensor_copy(
                    cstate_c[:, dt:dt + 1], tcol(vbuf, 0)[:, TB - 1:TB]),
                    waits=[t_sc0])
                carry_tok_c[dt] = [t_cs0]
                sc_prev = [t_sc0, t_cs0]
                vb_busy[0] = [t_sc0, t_cs0]
                vb_busy[2] = [t_sc0]

            t_init = vop(lambda e: e.tensor_scalar(
                cstate[:], cstate_c[:], c_maskcol[:], None, ALU.mult),
                waits=em.fence())

            for mc in range(NCH):
                pair_ready = load_and_ln1(x_fm, mc)
                si_toks, gt_toks = [], []
                for dt in range(NT):
                    bank = (pA, pB, pC, pD)[dt % 4]
                    tok_i = proj_weight("wi", dt, pair_ready, bank)
                    t_si = aop(lambda e, dt=dt, b=bank: e.activation(
                        out=tcol(vbuf, dt), in_=b[:], func=AF.Silu), waits=[tok_i])
                    psum_release(bank, t_si)
                    si_toks.append(t_si)
                for dt in range(NT):
                    bank = (pA, pB, pC, pD)[dt % 4]
                    tok_f = proj_weight("wf", dt, pair_ready, bank)
                    t_gt = aop(lambda e, dt=dt, b=bank: e.activation(
                        out=tcol(xbuf, dt), in_=b[:], func=AF.Sigmoid),
                        waits=[tok_f])
                    psum_release(bank, t_gt)
                    gt_toks.append(t_gt)
                scan_prev = [t_init]
                o_done = []
                for dt in range(NT):
                    t_v1 = vop(lambda e, dt=dt: e.tensor_mul(
                        sqs[:], tcol(vbuf, dt), tcol(xbuf, dt)),
                        waits=[si_toks[dt], gt_toks[dt]])
                    t_v2 = vop(lambda e, dt=dt: e.tensor_sub(
                        tcol(vbuf, dt), tcol(vbuf, dt), sqs[:]), waits=[t_v1])
                    t_sc = vop(lambda e, dt=dt: e.tensor_tensor_scan(
                        tcol(vbuf, dt), tcol(xbuf, dt), tcol(vbuf, dt),
                        cstate[:, dt:dt + 1], ALU.mult, ALU.add),
                        waits=[t_v2] + scan_prev + carry_tok[dt])
                    t_cs = gop(lambda e, dt=dt: e.tensor_copy(
                        cstate[:, dt:dt + 1], tcol(vbuf, dt)[:, TB - 1:TB]),
                        waits=[t_sc])
                    carry_tok[dt] = [t_cs]
                    scan_prev = [t_sc, t_cs]
                    o_done.append([t_sc, t_cs])

                sg_toks = []
                for dt in range(NT):
                    bank = (pA, pB, pC, pD)[dt % 4]
                    tok_g = proj_weight("wg", dt, pair_ready, bank)
                    t_sg = aop(lambda e, dt=dt, b=bank: e.activation(
                        out=tcol(xbuf, dt), in_=b[:], func=AF.Silu),
                        waits=[tok_g] + o_done[dt])
                    psum_release(bank, t_sg)
                    sg_toks.append(t_sg)

                acc = StatsAcc(scrA, NT, want_sum=False)
                for j in range(NT):
                    acc.tile(tcol(vbuf, j), o_done[j])
                _, tok_sq = acc.finish()
                t_rq = aop(lambda e: e.activation(out=g_rowf(R_T0),
                           in_=pF[0:1, :], func=AF.Copy, bias=EPS, scale=INV_D),
                           waits=[tok_sq] + rows_busy[0])
                psum_release(pF, t_rq)
                t_rc = vop(lambda e: e.reciprocal(g_rowf(R_NM),
                           g_rowf(R_T0)), waits=[t_rq] + rows_busy[0])
                t_rms = aop(lambda e: e.activation(out=g_rowf(R_S),
                            in_=g_rowf(R_NM), func=AF.Sqrt),
                            waits=[t_rc] + rows_busy[0])
                rows_busy[0] = [t_rms, t_rc]
                t_rbc = bcast(g_rowf(R_S), sbc, [t_rms])

                og_pr = []
                prev_o3 = []
                for dt in range(NT):
                    t_o2 = vop(lambda e, dt=dt: e.tensor_scalar(
                        sqs[:], tcol(vbuf, dt), c_gnw[:, dt:dt + 1], None,
                        ALU.mult), waits=o_done[dt] + prev_o3)
                    t_o3 = vop(lambda e, dt=dt: e.tensor_mul(
                        tcol(vbuf, dt), sqs[:], tcol(xbuf, dt)),
                        waits=[t_o2, sg_toks[dt]])
                    t_h = aop(lambda e, dt=dt: e.copy(out=tcol(oghi, dt),
                              in_=tcol(vbuf, dt)), waits=[t_o3])
                    t_l = gop(lambda e, dt=dt: e.tensor_sub(
                        tcol(oglo, dt), tcol(vbuf, dt), tcol(oghi, dt)),
                        waits=[t_o3, t_h])
                    prev_o3 = [t_o3]
                    og_pr.append([t_h, t_l])

                acc2 = StatsAcc(scrA, NT)
                for ht in range(NT):
                    bank = (pA, pB, pC, pD)[ht % 4]
                    conv_pump(4)
                    slot, ready, done_cb = wstream("wo", ht)
                    tok = mm3_group(
                        bank, NK,
                        lambda k, s=slot: whi[s][:, k * P:(k + 1) * P],
                        lambda k, s=slot: wlo[s][:, k * P:(k + 1) * P],
                        lambda k: tcol(oghi, k),
                        lambda k: tcol(oglo, k),
                        ready,
                        rhs_waits_f=lambda k: og_pr[k])
                    done_cb(tok)
                    t_ev = vop(lambda e, ht=ht, b=bank: e.tensor_mul(
                        tcol(xbuf, ht), b[:], sbc[:]), waits=[tok, t_rbc])
                    psum_release(bank, t_ev)
                    acc2.tile(tcol(xbuf, ht), [t_ev])
                tok_sum, tok_sq = acc2.finish()
                t_s, t_nm = ln_rows(scrA, tok_sum, tok_sq)
                t_sbc = bcast(g_rowf(R_S), sbc, [t_s])
                t_nmbc = bcast(g_rowf(R_NM), nmbc, [t_nm])

                acc3 = StatsAcc(scrA, NT)
                xr_busy = [[], []]
                prev_a = []
                prev_sp = []
                for j in range(NT):
                    xr = xr0 if j % 2 == 0 else xr1
                    t_xr = dma("gpsimd", "sXR0" if j % 2 == 0 else "sXR1", xr[:],
                               x_fm[j * P:(j + 1) * P, mc * TB:(mc + 1) * TB],
                               waits=xr_busy[j % 2])
                    t_a1 = vop(lambda e, j=j: e.tensor_mul(sqs[:], tcol(xbuf, j),
                                                           sbc[:]),
                               waits=[t_sbc, t_nmbc] + prev_a)
                    t_a2 = vop(lambda e: e.tensor_add(sqs[:], sqs[:], nmbc[:]),
                               waits=[t_a1])
                    t_a3 = vop(lambda e, xr=xr: e.tensor_add(x2t[:], sqs[:], xr[:]),
                               waits=[t_a2, t_xr] + prev_sp)
                    t_sp = dma("gpsimd", "sSP",
                               x2s[j * P:(j + 1) * P, mc * TB:(mc + 1) * TB],
                               x2t[:], waits=[t_a3])
                    rd = acc3.tile(x2t[:], [t_a3])
                    xr_busy[j % 2] = [t_xr] + rd
                    prev_a = [t_a3]
                    prev_sp = [t_sp] + rd
                tok_sum, tok_sq = acc3.finish()
                t_s, t_nm = ln_rows(scrA, tok_sum, tok_sq)
                t_c1 = dma("gpsimd", "sSP", s3row[0:1, mc * TB:(mc + 1) * TB],
                           g_rowf(R_S), waits=[t_s])
                t_c2 = dma("gpsimd", "sSP", nm3row[0:1, mc * TB:(mc + 1) * TB],
                           g_rowf(R_NM), waits=[t_nm])
                rows_busy[0] = rows_busy[0] + [t_c1, t_c2]

            conv_pump(len(conv_queue))
        fence_attn = em.fence()
        with ExitStack() as ffn_ctx:
            def fsb(name, shape, dt=F32):
                return ffn_ctx.enter_context(nc.sbuf_tensor(name, shape, dt))
            zbuf = fsb("zbuf", [P, NDH * TB])
            whi[2] = fsb("whi2", [P, 2048], BF16)
            wlo[2] = fsb("wlo2", [P, 2048], BF16)
            wnslots[0] = 3
            wslot_busy[2] = list(fence_attn)
            freg = fsb("freg", [P, 2 * NT * TB], BF16)
            hidf = fsb("hidf", [P, TB])
            hidhi = fsb("hidhi", [P, TB], BF16)
            hidlo = fsb("hidlo", [P, TB], BF16)

            fr0 = fsb("fr0", [P, TB])[:]
            fr1 = fr0
            yt = sqs[:]

            flnhi = freg[:, 0:NT * TB]
            flnlo = freg[:, NT * TB:2 * NT * TB]
            fb16 = freg
            fb_base = 2 * (4 * TB + 6 * TB)

            def fb(i):
                return fb16[:, fb_base + i * TB: fb_base + (i + 1) * TB]

            f_tshi = [fb(0), fb(1)]
            f_tslo = [fb(2), fb(3)]
            f_tshi2 = [fb(4), fb(5)]
            f_tslo2 = [fb(6), fb(7)]
            zhi = fb(8)
            zlo = fb(9)
            scrF = Scr(f_tshi, f_tslo, f_tshi2, f_tslo2, g_rowf, [sgs, sgs])

            for fc in range(NCH):
                fw = fence_attn if fc == 0 else em.fence()
                t_r1 = dma("gpsimd", "sRL0", g_rowf(R_S),
                           s3row[0:1, fc * TB:(fc + 1) * TB],
                           waits=fw + rows_busy[0])
                t_r2 = dma("gpsimd", "sRL1", g_rowf(R_NM),
                           nm3row[0:1, fc * TB:(fc + 1) * TB],
                           waits=fw + rows_busy[0])
                rows_busy[0] = [t_r1, t_r2]
                t_sbc = bcast(g_rowf(R_S), sbc, fw + [t_r1])
                t_nmbc = bcast(g_rowf(R_NM), nmbc, fw + [t_r2])
                allpr = []
                hid_busy = [[], []]
                prev_t3 = []
                prev_t2 = []
                prev_t4 = []
                for j in range(NT):
                    t_ld = dma("gpsimd", "sF1", hidf[:],
                               x2s[j * P:(j + 1) * P, fc * TB:(fc + 1) * TB],
                               waits=fw + hid_busy[0])
                    t1 = vop(lambda e: e.tensor_mul(sqs[:], hidf[:], sbc[:]),
                             waits=[t_ld, t_sbc, t_nmbc] + prev_t2)
                    t2 = vop(lambda e: e.tensor_add(sgs[:], sqs[:], nmbc[:]),
                             waits=[t1] + prev_t3 + prev_t4)
                    t3 = aop(lambda e, j=j: e.copy(out=tcol(flnhi, j), in_=sgs[:]),
                             waits=[t2] + (fw if j == 0 else []))
                    t4 = gop(lambda e, j=j: e.tensor_sub(tcol(flnlo, j), sgs[:],
                                                         tcol(flnhi, j)),
                             waits=[t2, t3])
                    prev_t3 = [t3]
                    prev_t2 = [t2]
                    prev_t4 = [t4]
                    hid_busy[0] = [t1]
                    sqs_last = [t1, t2]
                    allpr.append([t3, t4])

                z_toks = [None] * NDH
                prev_hid = list(hid_busy[0])
                last_w13 = []
                phys = list(range(NDH))
                spare_phys = [NDH]

                def zap(pi):
                    return tcol(zbuf, pi) if pi < NDH else sqs[:]

                bf_tok = {i: [] for i in range(NDH)}
                bf_tok[NDH] = list(sqs_last)

                def bf_progress(j):
                    h = 1
                    while h < NDH and (j + 1) % (2 * h) == 0:
                        base = j + 1 - 2 * h
                        for i in range(base, base + h):
                            a, b = i, i + h
                            pa, pb_ = phys[a], phys[b]
                            sp = spare_phys[0]
                            w = bf_tok[pa] + bf_tok[pb_] + bf_tok[sp]
                            t_s_ = gop(lambda e, sp=sp, pa=pa, pb=pb_: e.tensor_add(
                                zap(sp), zap(pa), zap(pb)), waits=w)
                            t_d_ = gop(lambda e, pa=pa, pb=pb_: e.tensor_sub(
                                zap(pb), zap(pa), zap(pb)), waits=w)
                            phys[a], spare_phys[0] = sp, pa
                            bf_tok[phys[a]] = [t_s_]
                            bf_tok[pb_] = [t_d_]
                            bf_tok[spare_phys[0]] = [t_s_, t_d_]
                        h *= 2

                for dh in range(NDH):
                    slot1, ready1, done1 = wstream("w1", dh)
                    tok1 = mm3_group(
                        pA, NK,
                        lambda k, s=slot1: whi[s][:, k * P:(k + 1) * P],
                        lambda k, s=slot1: wlo[s][:, k * P:(k + 1) * P],
                        lambda k: tcol(flnhi, k),
                        lambda k: tcol(flnlo, k),
                        ready1,
                        rhs_waits_f=lambda k: allpr[k])
                    done1(tok1)
                    slot3, ready3, done3 = wstream("w3", dh)
                    tok3 = mm3_group(
                        pB, NK,
                        lambda k, s=slot3: whi[s][:, k * P:(k + 1) * P],
                        lambda k, s=slot3: wlo[s][:, k * P:(k + 1) * P],
                        lambda k: tcol(flnhi, k),
                        lambda k: tcol(flnlo, k),
                        ready3,
                        rhs_waits_f=lambda k: allpr[k])
                    done3(tok3)
                    t_su = aop(lambda e: e.activation(out=sgs[:], in_=pA[:],
                               func=AF.Silu), waits=[tok1] + prev_hid)
                    psum_release(pA, t_su)
                    t_hd = vop(lambda e: e.scalar_tensor_tensor(
                        hidf[:], pB[:], SC_H, sgs[:], ALU.mult, ALU.mult),
                        waits=[tok3, t_su] + prev_hid)
                    psum_release(pB, t_hd)
                    t_hh = aop(lambda e: e.copy(out=hidhi[:], in_=hidf[:]),
                               waits=[t_hd])
                    t_hl = vop(lambda e: e.tensor_sub(hidlo[:], hidf[:], hidhi[:]),
                               waits=[t_hd, t_hh])
                    pfC = psum_free[id(pC)]
                    top(lambda e: e.matmul(pC[:], c_h128[:], hidhi[:],
                        start=True, stop=False),
                        waits=[t_hh, t_hl] + pfC + statics_w, inc=False)
                    t_hm = top(lambda e: e.matmul(pC[:], c_h128[:], hidlo[:],
                               start=False, stop=True))
                    t_ze = aop(lambda e, dh=dh: e.copy(out=tcol(zbuf, dh), in_=pC[:]),
                               waits=[t_hm])
                    psum_release(pC, t_ze)
                    z_toks[dh] = t_ze
                    bf_tok[dh] = [t_ze]
                    bf_progress(dh)
                    prev_hid = [t_hm]
                    last_w13 = [tok1, tok3]

                lnpair_done = last_w13

                out_banks = [pA, pB, pC, pD]
                acc4 = StatsAcc(scrF, NT, single=True)
                ev_prev = prev_hid
                fr_busy = [list(lnpair_done), list(lnpair_done)]
                zpairs = [(zhi, zlo), (hidhi[:], hidlo[:])]
                zp_busy = [list(lnpair_done), list(lnpair_done) + prev_hid]
                for g in range(NG):
                    grp_last = [None] * 4
                    for hp in range(NDH):
                        slot, ready, done_cb = wstream("w2", g * NDH + hp)
                        pz = phys[hp]
                        zp = (g * NDH + hp) % 2
                        zh_, zl_ = zpairs[zp]
                        t_zh = aop(lambda e, pz=pz, zh=zh_: e.copy(
                            out=zh, in_=zap(pz)),
                            waits=bf_tok[pz] + zp_busy[zp])
                        t_zl = vop(lambda e, pz=pz, zh=zh_, zl=zl_: e.tensor_sub(
                            zl, zap(pz), zh),
                            waits=bf_tok[pz] + [t_zh] + zp_busy[zp])
                        for q in range(4):
                            bank = out_banks[q]
                            pf = psum_free[id(bank)] if hp == 0 else []
                            w0 = ready + [t_zh, t_zl] + pf
                            top(lambda e, b=bank, s=slot, q=q, zh=zh_,
                                st=(hp == 0): e.matmul(
                                b[:], whi[s][:, q * P:(q + 1) * P], zh,
                                start=st, stop=False), waits=w0, inc=False)
                            top(lambda e, b=bank, s=slot, q=q, zl=zl_: e.matmul(
                                b[:], whi[s][:, q * P:(q + 1) * P], zl,
                                start=False, stop=False), inc=False)
                            grp_last[q] = top(lambda e, b=bank, s=slot, q=q,
                                              zh=zh_, sp=(hp == NDH - 1): e.matmul(
                                b[:], wlo[s][:, q * P:(q + 1) * P], zh,
                                start=False, stop=sp))
                        done_cb(grp_last[3])
                        zp_busy[zp] = [grp_last[3]]
                    for q in range(4):
                        dt2 = g * 4 + q
                        xr = fr0
                        t_ev = vop(lambda e, b=out_banks[q]: e.tensor_copy(
                            hidf[:], b[:]), waits=[grp_last[q]] + ev_prev)
                        psum_release(out_banks[q], t_ev)
                        t_sp = dma("gpsimd", "sFS", fs[dt2 * P:(dt2 + 1) * P, :],
                                   hidf[:], waits=[t_ev])
                        t_ld = dma("gpsimd", "sFR0",
                                   xr, fs[dt2 * P:(dt2 + 1) * P, :],
                                   waits=[t_sp] + fr_busy[0] + fr_busy[1])
                        rd = acc4.tile(xr, [t_ld])
                        fr_busy[0] = rd
                        fr_busy[1] = rd
                        ev_prev = [t_sp]
                tok_sum, tok_sq = acc4.finish()
                t_s, t_nm = ln_rows(scrF, tok_sum, tok_sq)
                t_sbc = bcast(g_rowf(R_S), sbc, [t_s])
                t_nmbc = bcast(g_rowf(R_NM), nmbc, [t_nm])

                prev_y = []
                hid_b = list(ev_prev)
                for j in range(NT):
                    xr = fr0
                    t_f = dma("gpsimd", "sFR0",
                              xr, fs[j * P:(j + 1) * P, :],
                              waits=fr_busy[0] + fr_busy[1] + prev_y)
                    t_x2 = dma("gpsimd", "sX2Y", hidf[:],
                               x2s[j * P:(j + 1) * P, fc * TB:(fc + 1) * TB],
                               waits=hid_b + prev_y)
                    t_y1 = vop(lambda e, xr=xr: e.tensor_mul(sqs[:], xr, sbc[:]),
                               waits=[t_f, t_sbc, t_nmbc] + prev_y)
                    t_y2 = vop(lambda e: e.tensor_add(sgs[:], sqs[:], nmbc[:]),
                               waits=[t_y1])
                    t_y3 = vop(lambda e: e.tensor_add(yt, sgs[:], hidf[:]),
                               waits=[t_y2, t_x2] + prev_y)
                    t_out = dma("gpsimd", "sO",
                                y_fm[j * P:(j + 1) * P, fc * TB:(fc + 1) * TB],
                                yt, waits=[t_y3])
                    fr_busy[0] = [t_y1]
                    fr_busy[1] = [t_y1]
                    hid_b = [t_y3]
                    prev_y = [t_y3, t_out]

            n_out = em.cur("sO")
            em.emit("gpsimd", lambda e: e.nop(), waits=[("sO", n_out)])

        with ExitStack() as sem_ctx:
            sems = {s: sem_ctx.enter_context(nc.semaphore(s)) for s in SEM_NAMES}
            block = sem_ctx.enter_context(nc.Block())

            def run(eng_name, eng):
                for waits, fn, inc in em.ops[eng_name]:
                    for s, v in waits:
                        eng.wait_ge(sems[s], v)
                    inst = fn(eng)
                    if inc is not None:
                        s, amt = inc
                        inst.then_inc(sems[s], amt)

            @block.sync
            def _(sync):
                run("sync", sync)

            @block.tensor
            def _(tensor):
                run("tensor", tensor)

            @block.vector
            def _(vector):
                run("vector", vector)

            @block.scalar
            def _(scalar):
                run("scalar", scalar)

            @block.gpsimd
            def _(gpsimd):
                run("gpsimd", gpsimd)

    return nc


_NC_CACHE = None


def _get_nc():
    global _NC_CACHE
    if _NC_CACHE is None:
        _NC_CACHE = build_nc()
    return _NC_CACHE


def _tile_lhsT(w, nk, nm):
    t = w.T.reshape(nk, P, nm, P).transpose(2, 1, 0, 3).reshape(nm * P, nk * P)
    return np.ascontiguousarray(t)


def kernel(L0_attn_hidden_states, L0_attn_Wi, L0_attn_Wf, L0_attn_Wg,
           L0_attn_Wo, L0_attn_gnorm_w, L0_ffn_W1, L0_ffn_W2, L0_ffn_W3):
    import ml_dtypes
    x = np.ascontiguousarray(np.asarray(L0_attn_hidden_states, dtype=np.float32))
    wi = _tile_lhsT(np.asarray(L0_attn_Wi, dtype=np.float32), NK, NT)
    wf = _tile_lhsT(np.asarray(L0_attn_Wf, dtype=np.float32), NK, NT)
    wg = _tile_lhsT(np.asarray(L0_attn_Wg, dtype=np.float32), NK, NT)
    wo = _tile_lhsT(np.asarray(L0_attn_Wo, dtype=np.float32), NK, NT)
    w1 = _tile_lhsT(np.asarray(L0_ffn_W1, dtype=np.float32), NK, NDH)
    w3 = _tile_lhsT(np.asarray(L0_ffn_W3, dtype=np.float32), NK, NDH)
    w2 = np.ascontiguousarray(
        np.asarray(L0_ffn_W2, dtype=np.float32).T.reshape(NDH, P, NG, TB)
        .transpose(2, 0, 1, 3).reshape(NG * NDH * P, TB))
    gnw = np.ascontiguousarray(
        np.asarray(L0_attn_gnorm_w, dtype=np.float32).reshape(NT, P).T)
    i = np.arange(P, dtype=np.uint32)
    anded = i[:, None] & i[None, :]
    pop = np.zeros((P, P), dtype=np.int64)
    for b in range(7):
        pop += (anded >> b) & 1
    h128 = np.where(pop % 2 == 0, 1.0, -1.0).astype(ml_dtypes.bfloat16)
    onescol = np.ones((P, 1), dtype=ml_dtypes.bfloat16)
    onesrow = np.ones((1, P), dtype=np.float32)

    nc = _get_nc()
    in_maps = []
    for c in range(8):
        b, half = c // 2, c % 2
        xm = np.ascontiguousarray(x[b, half * TCORE:(half + 1) * TCORE, :].T)
        xc = np.ascontiguousarray(x[b, 0:TCORE, :].T)
        mask = np.full((P, 1), float(half), dtype=np.float32)
        in_maps.append({
            "x_fm": xm, "xc_fm": xc, "wi": wi, "wf": wf, "wg": wg, "wo": wo,
            "w1": w1, "w3": w3, "w2": w2, "gnw": gnw, "h128": h128,
            "onescol": onescol, "onesrow": onesrow, "maskcol": mask,
        })
    res = run_bass_kernel_spmd(nc, in_maps, core_ids=list(range(8)))
    out = np.empty_like(x)
    for c in range(8):
        b, half = c // 2, c % 2
        out[b, half * TCORE:(half + 1) * TCORE, :] = res.results[c]["y_fm"].T
    return out


# version 21
# speedup vs baseline: 1.0557x; 1.0557x over previous
import numpy as np
from contextlib import ExitStack
import concourse.bass as bass
import concourse.mybir as mybir
from concourse.bass_utils import run_bass_kernel_spmd

F32 = mybir.dt.float32
BF16 = mybir.dt.bfloat16
ALU = mybir.AluOpType
AF = mybir.ActivationFunctionType

P = 128
D = 2048
HID = 8192
TCORE = 1024
TB = 512
NCH = TCORE // TB
NT = D // P
NK = D // P
NDH = HID // P
NG = 4
INV_D = 1.0 / D
EPS = 1e-5
SC_H = float(1.0 / np.sqrt(HID))

ENGINES = ("sync", "tensor", "vector", "scalar", "gpsimd")
DMA_SEMS = {"sC", "sWH0", "sWH1", "sWH2", "sWL0", "sWL1", "sWL2",
            "sCV0", "sCV1", "sCVH0", "sCVH1", "sCVL0", "sCVL1",
            "sL", "sXR0", "sXR1", "sSP", "sRL0", "sRL1",
            "sF1", "sFS", "sFR0", "sFR1", "sX2Y", "sO"}
SEM_NAMES = tuple(sorted(DMA_SEMS)) + ("s_t", "s_v", "s_a", "s_g")


class Em:

    def __init__(self):
        self.ops = {e: [] for e in ENGINES}
        self.cnt = {s: 0 for s in SEM_NAMES}
        self.own = {e: {s: 0 for s in SEM_NAMES} for e in ENGINES}
        self.waited = {e: {s: 0 for s in SEM_NAMES} for e in ENGINES}

    def emit(self, eng, fn, waits=(), inc=None):
        best = {}
        for s, v in waits:
            if v > best.get(s, 0):
                best[s] = v
        real = []
        for s, v in best.items():
            if self.waited[eng][s] >= v:
                continue
            real.append((s, v))
            self.waited[eng][s] = v
        tok = None
        if inc is not None:
            s, amt = inc
            self.cnt[s] += amt
            self.own[eng][s] += amt
            tok = (s, self.cnt[s])
        self.ops[eng].append((tuple(real), fn, inc))
        return tok

    def cur(self, sem):
        return self.cnt[sem]

    def fence(self):
        return [(s, self.cnt[s]) for s in SEM_NAMES]


class Scr:

    def __init__(self, tshi, tslo, tshi2, tslo2, rowf, sqq):
        self.tshi, self.tslo = tshi, tslo
        self.tshi2, self.tslo2 = tshi2, tslo2
        self.rowf = rowf
        self.sqq = sqq

R_M, R_S, R_NM, R_T0 = range(4)


def build_nc():
    nc = bass.Bass("TRN2", target_bir_lowering=False, debug=False)

    def din(name, shape, dt=F32):
        return nc.dram_tensor(name, shape, dt, kind="ExternalInput").ap()

    x_fm = din("x_fm", [D, TCORE])
    xc_fm = din("xc_fm", [D, TCORE])
    wi = din("wi", [D, D])
    wf = din("wf", [D, D])
    wg = din("wg", [D, D])
    wo = din("wo", [D, D])
    w1 = din("w1", [HID, D])
    w3 = din("w3", [HID, D])
    w2 = din("w2", [NG * NDH * P, TB])
    gnw = din("gnw", [P, NT])
    h128 = din("h128", [P, P], BF16)
    onescol = din("onescol", [P, 1], BF16)
    onesrow = din("onesrow", [1, P])
    maskcol = din("maskcol", [P, 1])
    wpair_dram = {}
    for nm, ap_, rows in (("wi", wi, D), ("wf", wf, D), ("wg", wg, D),
                          ("wo", wo, D), ("w1", w1, HID), ("w3", w3, HID),
                          ("w2", w2, NG * NDH * P)):
        cols = TB if nm == "w2" else D
        hi_t = nc.dram_tensor(nm + "_h", [rows, cols], BF16, kind="Internal").ap()
        lo_t = nc.dram_tensor(nm + "_l", [rows, cols], BF16, kind="Internal").ap()
        wpair_dram[nm] = (ap_, hi_t, lo_t, rows, cols)
    y_fm = nc.dram_tensor("y_fm", [D, TCORE], F32, kind="ExternalOutput").ap()
    x2s = nc.dram_tensor("x2s", [D, TCORE], F32, kind="Internal").ap()
    s3row = nc.dram_tensor("s3row", [1, TCORE], F32, kind="Internal").ap()
    nm3row = nc.dram_tensor("nm3row", [1, TCORE], F32, kind="Internal").ap()
    fs = nc.dram_tensor("fs", [D, TB], F32, kind="Internal").ap()

    em = Em()

    with ExitStack() as ctx:
        def sb(name, shape, dt=F32):
            return ctx.enter_context(nc.sbuf_tensor(name, shape, dt))

        def psum(name, shape):
            return ctx.enter_context(nc.psum_tensor(name, shape, F32))

        c_h128 = sb("c_h128", [P, P], BF16)
        c_onescol = sb("c_onescol", [P, 1], BF16)
        c_onesrow = sb("c_onesrow", [1, P])
        c_maskcol = sb("c_maskcol", [P, 1])
        c_gnw = sb("c_gnw", [P, NT])
        cstate = sb("cstate", [P, NT])
        cstate_c = sb("cstate_c", [P, NT])
        whi = [sb(f"whi{i}", [P, 2048], BF16) for i in range(2)] + [None]
        wlo = [sb(f"wlo{i}", [P, 2048], BF16) for i in range(2)] + [None]
        wnslots = [2]
        sqs = sb("sqs", [P, TB])
        sgs = sb("sgs", [P, TB])
        sbc = sb("sbc", [P, TB])
        nmbc = sb("nmbc", [P, TB])
        g_rows = sb("g_rows", [1, 4 * TB])

        def g_rowf(i):
            return g_rows[0:1, i * TB:(i + 1) * TB]

        pA = psum("pA", [P, TB])
        pB = psum("pB", [P, TB])
        pC = psum("pC", [P, TB])
        pD = psum("pD", [P, TB])
        pE = psum("pE", [P, TB])
        pF = psum("pF", [P, TB])
        psum_free = {id(p): [] for p in (pA, pB, pC, pD, pE, pF)}

        def dma(eng, sem, out_ap, in_ap, waits=()):
            selfw = [(sem, em.cur(sem))]
            return em.emit(eng, lambda e, o=out_ap, i=in_ap: e.dma_start(out=o, in_=i),
                           waits=list(waits) + selfw, inc=(sem, 16))

        def vop(fn, waits=()):
            return em.emit("vector", fn, waits=waits, inc=("s_v", 1))

        def aop(fn, waits=()):
            return em.emit("scalar", fn, waits=waits, inc=("s_a", 1))

        def gop(fn, waits=()):
            return em.emit("gpsimd", fn, waits=waits, inc=("s_g", 1))

        def top(fn, waits=(), inc=True):
            return em.emit("tensor", fn, waits=waits,
                           inc=(("s_t", 1) if inc else None))

        for dst, srcap in ((c_h128, h128), (c_onescol, onescol),
                           (c_onesrow, onesrow), (c_maskcol, maskcol),
                           (c_gnw, gnw)):
            dma("gpsimd", "sC", dst[:], srcap[:])
        t_memset = gop(lambda e: e.memset(cstate_c[:], 0.0))
        statics_w = [("sC", em.cur("sC")), t_memset]

        cw_holder, cwh_holder, cwl_holder = [], [], []

        class _H:
            def __init__(self, h, i):
                self.h = h
                self.i = i
            def __getitem__(self, k):
                return self.h[self.i][k]

        cw = [_H(cw_holder, 0), _H(cw_holder, 1)]
        cwh = [_H(cwh_holder, 0), _H(cwh_holder, 1)]
        cwl = [_H(cwl_holder, 0), _H(cwl_holder, 1)]

        conv_order = ("wi", "wf", "wg", "wo", "w1", "w3", "w2")
        conv_queue = []
        for nm in conv_order:
            _, _, _, rows, cols = wpair_dram[nm]
            for si in range(rows // P):
                for col0 in range(0, cols, 1024):
                    conv_queue.append((nm, si, col0, min(1024, cols - col0)))
        conv_tok = {}
        conv_part = {}
        conv_next = [0]
        conv_busy = [{"cw": [], "cwh": [], "cwl": []} for _ in range(2)]

        conv_in_tok = {}
        conv_in_next = [0]

        def _conv_emit_in(k):
            if k >= len(conv_queue) or k in conv_in_tok:
                return
            nm, si, col0, nc_ = conv_queue[k]
            p = k % 2
            src_w, _, _, rows, cols = wpair_dram[nm]
            rs = slice(si * P, (si + 1) * P)
            cs = slice(col0, col0 + nc_)
            conv_in_tok[k] = dma("gpsimd", "sCV0" if p == 0 else "sCV1",
                                 cw[p][:, 0:nc_], src_w[rs, cs],
                                 waits=conv_busy[p]["cw"])

        def conv_pump(n=1):
            for _ in range(n):
                k = conv_next[0]
                if k >= len(conv_queue):
                    return
                conv_next[0] += 1
                _conv_emit_in(k)
                _conv_emit_in(k + 1)
                nm, si, col0, nc_ = conv_queue[k]
                p = k % 2
                _, hi_t, lo_t, rows, cols = wpair_dram[nm]
                rs = slice(si * P, (si + 1) * P)
                cs = slice(col0, col0 + nc_)
                bz = conv_busy[p]
                t_in = conv_in_tok.pop(k)
                t_hi = aop(lambda e, p=p, c=nc_: e.copy(out=cwh[p][:, 0:c],
                           in_=cw[p][:, 0:c]),
                           waits=[t_in] + bz["cwh"])
                t_lo = gop(lambda e, p=p, c=nc_: e.tensor_sub(
                    cwl[p][:, 0:c], cw[p][:, 0:c], cwh[p][:, 0:c]),
                    waits=[t_in, t_hi] + bz["cwl"])
                t_oh = dma("gpsimd", "sCVH0" if p == 0 else "sCVH1",
                           hi_t[rs, cs], cwh[p][:, 0:nc_], waits=[t_hi])
                t_ol = dma("gpsimd", "sCVL0" if p == 0 else "sCVL1",
                           lo_t[rs, cs], cwl[p][:, 0:nc_], waits=[t_lo])
                bz["cw"] = [t_hi, t_lo]
                bz["cwh"] = [t_oh]
                bz["cwl"] = [t_ol]
                key = (nm, si)
                conv_part.setdefault(key, []).extend([t_oh, t_ol])
                if col0 + nc_ >= cols:
                    conv_tok[key] = conv_part.pop(key)

        def conv_require(nm, si):
            while (nm, si) not in conv_tok:
                conv_pump(1)
            return conv_tok[(nm, si)]

        wslot_busy = [[], [], []]
        wcounter = [0]

        def wstream(nm, si):
            _, hi_t, lo_t, rows, cols = wpair_dram[nm]
            cv = conv_require(nm, si)
            i = wcounter[0]
            wcounter[0] += 1
            slot = i % wnslots[0]
            rs = slice(si * P, (si + 1) * P)
            t_h = dma("sync", ("sWH0", "sWH1", "sWH2")[slot],
                      whi[slot][:, 0:cols], hi_t[rs, :],
                      waits=cv + wslot_busy[slot])
            t_l = dma("sync", ("sWL0", "sWL1", "sWL2")[slot],
                      wlo[slot][:, 0:cols], lo_t[rs, :],
                      waits=cv + wslot_busy[slot])

            def done_cb(mm_tok, s=slot):
                wslot_busy[s] = [mm_tok]
            return slot, [t_h, t_l], done_cb

        def mm3_group(bank, nk, lhsT_hi_f, lhsT_lo_f, rhs_hi_f, rhs_lo_f, waits,
                      rhs_waits_f=None):
            pf = psum_free[id(bank)]
            tok = None
            for k in range(nk):
                w0 = (list(waits) + pf) if k == 0 else []
                if rhs_waits_f is not None:
                    w0 = w0 + list(rhs_waits_f(k))
                first = (k == 0)
                last = (k == nk - 1)
                top(lambda e, a=lhsT_hi_f(k), b=rhs_hi_f(k), s=first: e.matmul(
                    bank[:], a, b, start=s, stop=False), waits=w0, inc=False)
                top(lambda e, a=lhsT_hi_f(k), b=rhs_lo_f(k): e.matmul(
                    bank[:], a, b, start=False, stop=False), inc=False)
                tok = top(lambda e, a=lhsT_lo_f(k), b=rhs_hi_f(k), s=last: e.matmul(
                    bank[:], a, b, start=False, stop=s))
            return tok

        def psum_release(bank, tok):
            psum_free[id(bank)] = [tok]

        stats_busy = [[]]
        rows_busy = [[]]

        class StatsAcc:
            def __init__(self, scr, ntiles, want_sum=True, single=False):
                self.scr = scr
                self.n = ntiles
                self.want_sum = want_sum
                self.single = single
                self.j = 0
                self.prev_sum = [list(stats_busy[0]), list(stats_busy[0])]
                self.prev_sq = [list(stats_busy[0]), list(stats_busy[0])]
                self.tok_sum = None
                self.tok_sq = None

            def tile(self, t_ap, waits):
                scr = self.scr
                j, n = self.j, self.n
                self.j += 1
                p = 0 if self.single else j % 2
                jmin = 1 if self.single else 2
                if self.want_sum:
                    ps = self.prev_sum[p] if j >= jmin else list(stats_busy[0])
                    t_hi = aop(lambda e, t=t_ap, p=p: e.copy(out=scr.tshi[p][:],
                               in_=t), waits=list(waits) + ps)
                    t_lo = vop(lambda e, t=t_ap, p=p: e.tensor_sub(scr.tslo[p][:],
                               t, scr.tshi[p][:]),
                               waits=list(waits) + [t_hi] + ps)
                    w0 = [t_hi, t_lo] + (psum_free[id(pE)] if j == 0 else [])
                    top(lambda e, s=(j == 0), p=p: e.matmul(pE[0:1, :],
                        c_onescol[:], scr.tshi[p][:], start=s, stop=False),
                        waits=w0, inc=False)
                    self.tok_sum = top(lambda e, s=(j == n - 1), p=p: e.matmul(
                        pE[0:1, :], c_onescol[:], scr.tslo[p][:], start=False,
                        stop=s))
                    self.prev_sum[p] = [self.tok_sum]
                pq = self.prev_sq[p] if j >= jmin else list(stats_busy[0])
                t_sq = vop(lambda e, t=t_ap, p=p: e.tensor_mul(scr.sqq[p][:], t, t),
                           waits=list(waits) + pq)
                t_hi2 = aop(lambda e, p=p: e.copy(out=scr.tshi2[p][:], in_=scr.sqq[p][:]),
                            waits=[t_sq] + pq)
                t_lo2 = vop(lambda e, p=p: e.tensor_sub(scr.tslo2[p][:], scr.sqq[p][:],
                            scr.tshi2[p][:]),
                            waits=[t_sq, t_hi2] + pq)
                w0 = [t_hi2, t_lo2] + (psum_free[id(pF)] if j == 0 else [])
                top(lambda e, s=(j == 0), p=p: e.matmul(pF[0:1, :], c_onescol[:],
                    scr.tshi2[p][:], start=s, stop=False), waits=w0, inc=False)
                self.tok_sq = top(lambda e, s=(j == n - 1), p=p: e.matmul(
                    pF[0:1, :], c_onescol[:], scr.tslo2[p][:], start=False, stop=s))
                self.prev_sq[p] = [self.tok_sq]
                return ([self.tok_sum] if self.want_sum else []) + [self.tok_sq]

            def finish(self):
                stats_busy[0] = (([self.tok_sum] if self.tok_sum else [])
                                 + [self.tok_sq])
                return self.tok_sum, self.tok_sq

        def ln_rows(scr, tok_sum, tok_sq, nfeat=D):
            rb = rows_busy[0]
            r = g_rowf
            inv = 1.0 / nfeat
            t_m = aop(lambda e: e.copy(out=r(R_M), in_=pE[0:1, :]),
                      waits=[tok_sum] + rb)
            t_m2 = aop(lambda e: e.mul(out=r(R_M), in_=r(R_M), mul=inv),
                       waits=[t_m])
            t_q = aop(lambda e: e.copy(out=r(R_T0), in_=pF[0:1, :]),
                      waits=[tok_sq] + rb)
            t_msq = vop(lambda e: e.tensor_mul(r(R_NM), r(R_M), r(R_M)),
                        waits=[t_m2] + rb)
            t_v = vop(lambda e: e.scalar_tensor_tensor(r(R_T0), r(R_T0), inv,
                      r(R_NM), ALU.mult, ALU.subtract), waits=[t_q, t_msq] + rb)
            t_ve = vop(lambda e: e.tensor_scalar_add(r(R_T0), r(R_T0), EPS),
                       waits=[t_v])
            t_rc = vop(lambda e: e.reciprocal(r(R_NM), r(R_T0)),
                       waits=[t_ve, t_msq] + rb)
            t_s = aop(lambda e: e.activation(out=r(R_S), in_=r(R_NM),
                      func=AF.Sqrt), waits=[t_rc] + rb)
            t_nm = vop(lambda e: e.scalar_tensor_tensor(r(R_NM), r(R_M), -1.0,
                       r(R_S), ALU.mult, ALU.mult), waits=[t_m2, t_s, t_rc] + rb)
            psum_release(pE, t_m)
            psum_release(pF, t_q)
            rows_busy[0] = [t_s, t_nm]
            return t_s, t_nm

        def bcast(row_ap, dst, waits):
            pf = psum_free[id(pD)]
            t_mm = top(lambda e, r=row_ap: e.matmul(pD[:], c_onesrow[:], r,
                                                    start=True, stop=True),
                       waits=list(waits) + pf + statics_w)
            t_ev = vop(lambda e, d=dst: e.tensor_copy(d[:], pD[:]), waits=[t_mm])
            psum_release(pD, t_ev)
            rows_busy[0] = rows_busy[0] + [t_mm]
            return t_ev

        def tcol(buf, j, width=TB):
            return buf[:, j * width:(j + 1) * width]

        with ExitStack() as attn_ctx:
            def asb(name, shape, dt=F32):
                return attn_ctx.enter_context(nc.sbuf_tensor(name, shape, dt))
            xbuf = asb("xbuf", [P, NT * TB])
            vbuf = asb("vbuf", [P, NT * TB])
            lnhi = asb("lnhi", [P, NT * TB], BF16)
            lnlo = asb("lnlo", [P, NT * TB], BF16)
            oghi = asb("oghi", [P, NT * TB], BF16)
            oglo = asb("oglo", [P, NT * TB], BF16)
            xr0 = asb("xr0", [P, TB])
            xr1 = asb("xr1", [P, TB])
            x2t = asb("x2t", [P, TB])
            a_tshi = [asb(f"a_tshi{i}", [P, TB], BF16) for i in range(2)]
            a_tslo = [asb(f"a_tslo{i}", [P, TB], BF16) for i in range(2)]
            a_tshi2 = [asb(f"a_tshi2{i}", [P, TB], BF16) for i in range(2)]
            a_tslo2 = [asb(f"a_tslo2{i}", [P, TB], BF16) for i in range(2)]
            a_sqq = [asb(f"a_sqq{i}", [P, TB]) for i in range(2)]
            for i in range(2):
                cw_holder.append(asb(f"cw{i}", [P, 1024]))
                cwh_holder.append(asb(f"cwh{i}", [P, 1024], BF16))
                cwl_holder.append(asb(f"cwl{i}", [P, 1024], BF16))
            conv_pump(16)
            scrA = Scr(a_tshi, a_tslo, a_tshi2, a_tslo2, g_rowf, a_sqq)

            def load_and_ln1(src_dram, mc, hi_buf=None, lo_buf=None):
                if hi_buf is None:
                    hi_buf, lo_buf = lnhi, lnlo
                fw = em.fence()
                src_chunk = src_dram[:, mc * TB:(mc + 1) * TB].rearrange(
                    "(j p) c -> p j c", p=P)
                dst_chunk = xbuf[:, 0:NT * TB].rearrange("p (j c) -> p j c", j=NT)
                t_ld = dma("gpsimd", "sL", dst_chunk, src_chunk, waits=fw)
                ld = [t_ld] * NT
                acc = StatsAcc(scrA, NT)
                for j in range(NT):
                    acc.tile(tcol(xbuf, j), [ld[j]])
                tok_sum, tok_sq = acc.finish()
                t_s, t_nm = ln_rows(scrA, tok_sum, tok_sq)
                t_sbc = bcast(g_rowf(R_S), sbc, [t_s])
                t_nmbc = bcast(g_rowf(R_NM), nmbc, [t_nm])
                pair_ready = []
                prev_t3 = []
                prev_t2 = []
                prev_t4 = []
                for j in range(NT):
                    t1 = vop(lambda e, j=j: e.tensor_mul(sqs[:], tcol(xbuf, j),
                                                         sbc[:]),
                             waits=[ld[j], t_nmbc, t_sbc] + prev_t2)
                    t2 = vop(lambda e: e.tensor_add(sgs[:], sqs[:], nmbc[:]),
                             waits=[t1] + prev_t3 + prev_t4)
                    t3 = aop(lambda e, j=j: e.copy(out=tcol(hi_buf, j), in_=sgs[:]),
                             waits=[t2])
                    t4 = gop(lambda e, j=j: e.tensor_sub(tcol(lo_buf, j), sgs[:],
                                                         tcol(hi_buf, j)),
                             waits=[t2, t3])
                    prev_t3 = [t3]
                    prev_t2 = [t2]
                    prev_t4 = [t4]
                    pair_ready.append([t3, t4])
                return pair_ready

            def proj_weight(wsrc, dt, pair_ready, bank):
                conv_pump(4)
                slot, ready, done_cb = wstream(wsrc, dt)
                tok = mm3_group(
                    bank, NK,
                    lambda k, s=slot: whi[s][:, k * P:(k + 1) * P],
                    lambda k, s=slot: wlo[s][:, k * P:(k + 1) * P],
                    lambda k: tcol(lnhi, k),
                    lambda k: tcol(lnlo, k),
                    ready,
                    rhs_waits_f=lambda k: pair_ready[k])
                done_cb(tok)
                return tok

            carry_tok_c = [[] for _ in range(NT)]
            carry_tok = [[] for _ in range(NT)]
            pr0 = load_and_ln1(xc_fm, 1)
            sc_prev = [t_memset]
            vb_busy = [[] for _ in range(4)]
            for dt in range(NT):
                conv_pump(6)
                slot, ready, done_i = wstream("wi", dt)
                tok_i0 = mm3_group(
                    pA, NK,
                    lambda k, s=slot: whi[s][:, k * P:(k + 1) * P],
                    lambda k, s=slot: wlo[s][:, k * P:(k + 1) * P],
                    lambda k: tcol(lnhi, k), lambda k: tcol(lnlo, k),
                    ready, rhs_waits_f=lambda k: pr0[k])
                done_i(tok_i0)
                t_si0 = aop(lambda e: e.activation(out=tcol(vbuf, 0), in_=pA[:],
                            func=AF.Silu), waits=[tok_i0] + vb_busy[0])
                psum_release(pA, t_si0)
                slot, ready, done_f = wstream("wf", dt)
                tok_f0 = mm3_group(
                    pC, NK,
                    lambda k, s=slot: whi[s][:, k * P:(k + 1) * P],
                    lambda k, s=slot: wlo[s][:, k * P:(k + 1) * P],
                    lambda k: tcol(lnhi, k), lambda k: tcol(lnlo, k),
                    ready, rhs_waits_f=lambda k: pr0[k])
                done_f(tok_f0)
                t_gt0 = aop(lambda e: e.activation(out=tcol(vbuf, 2), in_=pC[:],
                            func=AF.Sigmoid), waits=[tok_f0] + vb_busy[2])
                psum_release(pC, t_gt0)
                t_v1 = vop(lambda e: e.tensor_mul(sqs[:], tcol(vbuf, 0),
                           tcol(vbuf, 2)), waits=[t_si0, t_gt0])
                t_v2 = vop(lambda e: e.tensor_sub(tcol(vbuf, 0), tcol(vbuf, 0),
                           sqs[:]), waits=[t_v1])
                t_sc0 = vop(lambda e, dt=dt: e.tensor_tensor_scan(
                    tcol(vbuf, 0), tcol(vbuf, 2), tcol(vbuf, 0),
                    cstate_c[:, dt:dt + 1], ALU.mult, ALU.add),
                    waits=[t_v2, t_memset] + sc_prev)
                t_cs0 = gop(lambda e, dt=dt: e.tensor_copy(
                    cstate_c[:, dt:dt + 1], tcol(vbuf, 0)[:, TB - 1:TB]),
                    waits=[t_sc0])
                carry_tok_c[dt] = [t_cs0]
                sc_prev = [t_sc0, t_cs0]
                vb_busy[0] = [t_sc0, t_cs0]
                vb_busy[2] = [t_sc0]

            t_init = vop(lambda e: e.tensor_scalar(
                cstate[:], cstate_c[:], c_maskcol[:], None, ALU.mult),
                waits=em.fence())

            for mc in range(NCH):
                pair_ready = load_and_ln1(x_fm, mc)
                si_toks, gt_toks = [], []
                for dt in range(NT):
                    bank = pA if dt % 2 == 0 else pB
                    tok_i = proj_weight("wi", dt, pair_ready, bank)
                    t_si = aop(lambda e, dt=dt, b=bank: e.activation(
                        out=tcol(vbuf, dt), in_=b[:], func=AF.Silu), waits=[tok_i])
                    psum_release(bank, t_si)
                    si_toks.append(t_si)
                for dt in range(NT):
                    bank = pA if dt % 2 == 0 else pB
                    tok_f = proj_weight("wf", dt, pair_ready, bank)
                    t_gt = aop(lambda e, dt=dt, b=bank: e.activation(
                        out=tcol(xbuf, dt), in_=b[:], func=AF.Sigmoid),
                        waits=[tok_f])
                    psum_release(bank, t_gt)
                    gt_toks.append(t_gt)
                scan_prev = [t_init]
                o_done = []
                for dt in range(NT):
                    t_v1 = vop(lambda e, dt=dt: e.tensor_mul(
                        sqs[:], tcol(vbuf, dt), tcol(xbuf, dt)),
                        waits=[si_toks[dt], gt_toks[dt]])
                    t_v2 = vop(lambda e, dt=dt: e.tensor_sub(
                        tcol(vbuf, dt), tcol(vbuf, dt), sqs[:]), waits=[t_v1])
                    t_sc = vop(lambda e, dt=dt: e.tensor_tensor_scan(
                        tcol(vbuf, dt), tcol(xbuf, dt), tcol(vbuf, dt),
                        cstate[:, dt:dt + 1], ALU.mult, ALU.add),
                        waits=[t_v2] + scan_prev + carry_tok[dt])
                    t_cs = gop(lambda e, dt=dt: e.tensor_copy(
                        cstate[:, dt:dt + 1], tcol(vbuf, dt)[:, TB - 1:TB]),
                        waits=[t_sc])
                    carry_tok[dt] = [t_cs]
                    scan_prev = [t_sc, t_cs]
                    o_done.append([t_sc, t_cs])

                sg_toks = []
                for dt in range(NT):
                    bank = pA if dt % 2 == 0 else pB
                    tok_g = proj_weight("wg", dt, pair_ready, bank)
                    t_sg = aop(lambda e, dt=dt, b=bank: e.activation(
                        out=tcol(xbuf, dt), in_=b[:], func=AF.Silu),
                        waits=[tok_g] + o_done[dt])
                    psum_release(bank, t_sg)
                    sg_toks.append(t_sg)

                acc = StatsAcc(scrA, NT, want_sum=False)
                for j in range(NT):
                    acc.tile(tcol(vbuf, j), o_done[j])
                _, tok_sq = acc.finish()
                t_rq = aop(lambda e: e.activation(out=g_rowf(R_T0),
                           in_=pF[0:1, :], func=AF.Copy, bias=EPS, scale=INV_D),
                           waits=[tok_sq] + rows_busy[0])
                psum_release(pF, t_rq)
                t_rc = vop(lambda e: e.reciprocal(g_rowf(R_NM),
                           g_rowf(R_T0)), waits=[t_rq] + rows_busy[0])
                t_rms = aop(lambda e: e.activation(out=g_rowf(R_S),
                            in_=g_rowf(R_NM), func=AF.Sqrt),
                            waits=[t_rc] + rows_busy[0])
                rows_busy[0] = [t_rms, t_rc]
                t_rbc = bcast(g_rowf(R_S), sbc, [t_rms])

                og_pr = []
                prev_o3 = []
                for dt in range(NT):
                    t_o2 = vop(lambda e, dt=dt: e.tensor_scalar(
                        sqs[:], tcol(vbuf, dt), c_gnw[:, dt:dt + 1], None,
                        ALU.mult), waits=o_done[dt] + prev_o3)
                    t_o3 = vop(lambda e, dt=dt: e.tensor_mul(
                        tcol(vbuf, dt), sqs[:], tcol(xbuf, dt)),
                        waits=[t_o2, sg_toks[dt]])
                    t_h = aop(lambda e, dt=dt: e.copy(out=tcol(oghi, dt),
                              in_=tcol(vbuf, dt)), waits=[t_o3])
                    t_l = gop(lambda e, dt=dt: e.tensor_sub(
                        tcol(oglo, dt), tcol(vbuf, dt), tcol(oghi, dt)),
                        waits=[t_o3, t_h])
                    prev_o3 = [t_o3]
                    og_pr.append([t_h, t_l])

                acc2 = StatsAcc(scrA, NT)
                for ht in range(NT):
                    bank = pA if ht % 2 == 0 else pB
                    conv_pump(4)
                    slot, ready, done_cb = wstream("wo", ht)
                    tok = mm3_group(
                        bank, NK,
                        lambda k, s=slot: whi[s][:, k * P:(k + 1) * P],
                        lambda k, s=slot: wlo[s][:, k * P:(k + 1) * P],
                        lambda k: tcol(oghi, k),
                        lambda k: tcol(oglo, k),
                        ready,
                        rhs_waits_f=lambda k: og_pr[k])
                    done_cb(tok)
                    t_ev = vop(lambda e, ht=ht, b=bank: e.tensor_mul(
                        tcol(xbuf, ht), b[:], sbc[:]), waits=[tok, t_rbc])
                    psum_release(bank, t_ev)
                    acc2.tile(tcol(xbuf, ht), [t_ev])
                tok_sum, tok_sq = acc2.finish()
                t_s, t_nm = ln_rows(scrA, tok_sum, tok_sq)
                t_sbc = bcast(g_rowf(R_S), sbc, [t_s])
                t_nmbc = bcast(g_rowf(R_NM), nmbc, [t_nm])

                acc3 = StatsAcc(scrA, NT)
                xr_busy = [[], []]
                prev_a = []
                prev_sp = []
                for j in range(NT):
                    xr = xr0 if j % 2 == 0 else xr1
                    t_xr = dma("gpsimd", "sXR0" if j % 2 == 0 else "sXR1", xr[:],
                               x_fm[j * P:(j + 1) * P, mc * TB:(mc + 1) * TB],
                               waits=xr_busy[j % 2])
                    t_a1 = vop(lambda e, j=j: e.tensor_mul(sqs[:], tcol(xbuf, j),
                                                           sbc[:]),
                               waits=[t_sbc, t_nmbc] + prev_a)
                    t_a2 = vop(lambda e: e.tensor_add(sqs[:], sqs[:], nmbc[:]),
                               waits=[t_a1])
                    t_a3 = vop(lambda e, xr=xr: e.tensor_add(x2t[:], sqs[:], xr[:]),
                               waits=[t_a2, t_xr] + prev_sp)
                    t_sp = dma("gpsimd", "sSP",
                               x2s[j * P:(j + 1) * P, mc * TB:(mc + 1) * TB],
                               x2t[:], waits=[t_a3])
                    rd = acc3.tile(x2t[:], [t_a3])
                    xr_busy[j % 2] = [t_xr] + rd
                    prev_a = [t_a3]
                    prev_sp = [t_sp] + rd
                tok_sum, tok_sq = acc3.finish()
                t_s, t_nm = ln_rows(scrA, tok_sum, tok_sq)
                t_c1 = dma("gpsimd", "sSP", s3row[0:1, mc * TB:(mc + 1) * TB],
                           g_rowf(R_S), waits=[t_s])
                t_c2 = dma("gpsimd", "sSP", nm3row[0:1, mc * TB:(mc + 1) * TB],
                           g_rowf(R_NM), waits=[t_nm])
                rows_busy[0] = rows_busy[0] + [t_c1, t_c2]

            conv_pump(len(conv_queue))
        fence_attn = em.fence()
        with ExitStack() as ffn_ctx:
            def fsb(name, shape, dt=F32):
                return ffn_ctx.enter_context(nc.sbuf_tensor(name, shape, dt))
            zbuf = fsb("zbuf", [P, NDH * TB])
            whi[2] = fsb("whi2", [P, 2048], BF16)
            wlo[2] = fsb("wlo2", [P, 2048], BF16)
            wnslots[0] = 3
            wslot_busy[2] = list(fence_attn)
            freg = fsb("freg", [P, 2 * NT * TB], BF16)
            hidf = fsb("hidf", [P, TB])
            hidhi = fsb("hidhi", [P, TB], BF16)
            hidlo = fsb("hidlo", [P, TB], BF16)

            fr0 = fsb("fr0", [P, TB])[:]
            fr1 = fr0
            yt = sqs[:]

            flnhi = freg[:, 0:NT * TB]
            flnlo = freg[:, NT * TB:2 * NT * TB]
            fb16 = freg
            fb_base = 2 * (4 * TB + 6 * TB)

            def fb(i):
                return fb16[:, fb_base + i * TB: fb_base + (i + 1) * TB]

            f_tshi = [fb(0), fb(1)]
            f_tslo = [fb(2), fb(3)]
            f_tshi2 = [fb(4), fb(5)]
            f_tslo2 = [fb(6), fb(7)]
            zhi = fb(8)
            zlo = fb(9)
            scrF = Scr(f_tshi, f_tslo, f_tshi2, f_tslo2, g_rowf, [sgs, sgs])

            for fc in range(NCH):
                fw = fence_attn if fc == 0 else em.fence()
                t_r1 = dma("gpsimd", "sRL0", g_rowf(R_S),
                           s3row[0:1, fc * TB:(fc + 1) * TB],
                           waits=fw + rows_busy[0])
                t_r2 = dma("gpsimd", "sRL1", g_rowf(R_NM),
                           nm3row[0:1, fc * TB:(fc + 1) * TB],
                           waits=fw + rows_busy[0])
                rows_busy[0] = [t_r1, t_r2]
                t_sbc = bcast(g_rowf(R_S), sbc, fw + [t_r1])
                t_nmbc = bcast(g_rowf(R_NM), nmbc, fw + [t_r2])
                allpr = []
                hid_busy = [[], []]
                prev_t3 = []
                prev_t2 = []
                prev_t4 = []
                for j in range(NT):
                    t_ld = dma("gpsimd", "sF1", hidf[:],
                               x2s[j * P:(j + 1) * P, fc * TB:(fc + 1) * TB],
                               waits=fw + hid_busy[0])
                    t1 = vop(lambda e: e.tensor_mul(sqs[:], hidf[:], sbc[:]),
                             waits=[t_ld, t_sbc, t_nmbc] + prev_t2)
                    t2 = vop(lambda e: e.tensor_add(sgs[:], sqs[:], nmbc[:]),
                             waits=[t1] + prev_t3 + prev_t4)
                    t3 = aop(lambda e, j=j: e.copy(out=tcol(flnhi, j), in_=sgs[:]),
                             waits=[t2] + (fw if j == 0 else []))
                    t4 = gop(lambda e, j=j: e.tensor_sub(tcol(flnlo, j), sgs[:],
                                                         tcol(flnhi, j)),
                             waits=[t2, t3])
                    prev_t3 = [t3]
                    prev_t2 = [t2]
                    prev_t4 = [t4]
                    hid_busy[0] = [t1]
                    sqs_last = [t1, t2]
                    allpr.append([t3, t4])

                z_toks = [None] * NDH
                prev_hid = list(hid_busy[0])
                last_w13 = []
                phys = list(range(NDH))
                spare_phys = [NDH]

                def zap(pi):
                    return tcol(zbuf, pi) if pi < NDH else sqs[:]

                bf_tok = {i: [] for i in range(NDH)}
                bf_tok[NDH] = list(sqs_last)

                def bf_progress(j):
                    h = 1
                    while h < NDH and (j + 1) % (2 * h) == 0:
                        base = j + 1 - 2 * h
                        for i in range(base, base + h):
                            a, b = i, i + h
                            pa, pb_ = phys[a], phys[b]
                            sp = spare_phys[0]
                            w = bf_tok[pa] + bf_tok[pb_] + bf_tok[sp]
                            t_s_ = gop(lambda e, sp=sp, pa=pa, pb=pb_: e.tensor_add(
                                zap(sp), zap(pa), zap(pb)), waits=w)
                            t_d_ = gop(lambda e, pa=pa, pb=pb_: e.tensor_sub(
                                zap(pb), zap(pa), zap(pb)), waits=w)
                            phys[a], spare_phys[0] = sp, pa
                            bf_tok[phys[a]] = [t_s_]
                            bf_tok[pb_] = [t_d_]
                            bf_tok[spare_phys[0]] = [t_s_, t_d_]
                        h *= 2

                for dh in range(NDH):
                    slot1, ready1, done1 = wstream("w1", dh)
                    bk1 = (pA, pD)[dh % 2]
                    bk3 = (pB, pE)[dh % 2]
                    bkh = (pC, pF)[dh % 2]
                    tok1 = mm3_group(
                        bk1, NK,
                        lambda k, s=slot1: whi[s][:, k * P:(k + 1) * P],
                        lambda k, s=slot1: wlo[s][:, k * P:(k + 1) * P],
                        lambda k: tcol(flnhi, k),
                        lambda k: tcol(flnlo, k),
                        ready1,
                        rhs_waits_f=lambda k: allpr[k])
                    done1(tok1)
                    slot3, ready3, done3 = wstream("w3", dh)
                    tok3 = mm3_group(
                        bk3, NK,
                        lambda k, s=slot3: whi[s][:, k * P:(k + 1) * P],
                        lambda k, s=slot3: wlo[s][:, k * P:(k + 1) * P],
                        lambda k: tcol(flnhi, k),
                        lambda k: tcol(flnlo, k),
                        ready3,
                        rhs_waits_f=lambda k: allpr[k])
                    done3(tok3)
                    t_su = aop(lambda e, b=bk1: e.activation(out=sgs[:], in_=b[:],
                               func=AF.Silu), waits=[tok1] + prev_hid)
                    psum_release(bk1, t_su)
                    t_hd = vop(lambda e, b=bk3: e.scalar_tensor_tensor(
                        hidf[:], b[:], SC_H, sgs[:], ALU.mult, ALU.mult),
                        waits=[tok3, t_su] + prev_hid)
                    psum_release(bk3, t_hd)
                    t_hh = aop(lambda e: e.copy(out=hidhi[:], in_=hidf[:]),
                               waits=[t_hd])
                    t_hl = vop(lambda e: e.tensor_sub(hidlo[:], hidf[:], hidhi[:]),
                               waits=[t_hd, t_hh])
                    pfC = psum_free[id(bkh)]
                    top(lambda e, b=bkh: e.matmul(b[:], c_h128[:], hidhi[:],
                        start=True, stop=False),
                        waits=[t_hh, t_hl] + pfC + statics_w, inc=False)
                    t_hm = top(lambda e, b=bkh: e.matmul(b[:], c_h128[:], hidlo[:],
                               start=False, stop=True))
                    t_ze = aop(lambda e, dh=dh, b=bkh: e.copy(out=tcol(zbuf, dh),
                               in_=b[:]), waits=[t_hm])
                    psum_release(bkh, t_ze)
                    z_toks[dh] = t_ze
                    bf_tok[dh] = [t_ze]
                    bf_progress(dh)
                    prev_hid = [t_hm]
                    last_w13 = [tok1, tok3]

                lnpair_done = last_w13

                out_banks = [pA, pB, pC, pD]
                acc4 = StatsAcc(scrF, NT, single=True)
                ev_prev = prev_hid
                fr_busy = [list(lnpair_done), list(lnpair_done)]
                zpairs = [(zhi, zlo), (hidhi[:], hidlo[:])]
                zp_busy = [list(lnpair_done), list(lnpair_done) + prev_hid]
                for g in range(NG):
                    grp_last = [None] * 4
                    for hp in range(NDH):
                        slot, ready, done_cb = wstream("w2", g * NDH + hp)
                        pz = phys[hp]
                        zp = (g * NDH + hp) % 2
                        zh_, zl_ = zpairs[zp]
                        t_zh = aop(lambda e, pz=pz, zh=zh_: e.copy(
                            out=zh, in_=zap(pz)),
                            waits=bf_tok[pz] + zp_busy[zp])
                        t_zl = vop(lambda e, pz=pz, zh=zh_, zl=zl_: e.tensor_sub(
                            zl, zap(pz), zh),
                            waits=bf_tok[pz] + [t_zh] + zp_busy[zp])
                        for q in range(4):
                            bank = out_banks[q]
                            pf = psum_free[id(bank)] if hp == 0 else []
                            w0 = ready + [t_zh, t_zl] + pf
                            top(lambda e, b=bank, s=slot, q=q, zh=zh_,
                                st=(hp == 0): e.matmul(
                                b[:], whi[s][:, q * P:(q + 1) * P], zh,
                                start=st, stop=False), waits=w0, inc=False)
                            top(lambda e, b=bank, s=slot, q=q, zl=zl_: e.matmul(
                                b[:], whi[s][:, q * P:(q + 1) * P], zl,
                                start=False, stop=False), inc=False)
                            grp_last[q] = top(lambda e, b=bank, s=slot, q=q,
                                              zh=zh_, sp=(hp == NDH - 1): e.matmul(
                                b[:], wlo[s][:, q * P:(q + 1) * P], zh,
                                start=False, stop=sp))
                        done_cb(grp_last[3])
                        zp_busy[zp] = [grp_last[3]]
                    for q in range(4):
                        dt2 = g * 4 + q
                        xr = fr0
                        t_ev = vop(lambda e, b=out_banks[q]: e.tensor_copy(
                            hidf[:], b[:]), waits=[grp_last[q]] + ev_prev)
                        psum_release(out_banks[q], t_ev)
                        t_sp = dma("gpsimd", "sFS", fs[dt2 * P:(dt2 + 1) * P, :],
                                   hidf[:], waits=[t_ev])
                        t_ld = dma("gpsimd", "sFR0",
                                   xr, fs[dt2 * P:(dt2 + 1) * P, :],
                                   waits=[t_sp] + fr_busy[0] + fr_busy[1])
                        rd = acc4.tile(xr, [t_ld])
                        fr_busy[0] = rd
                        fr_busy[1] = rd
                        ev_prev = [t_sp]
                tok_sum, tok_sq = acc4.finish()
                t_s, t_nm = ln_rows(scrF, tok_sum, tok_sq)
                t_sbc = bcast(g_rowf(R_S), sbc, [t_s])
                t_nmbc = bcast(g_rowf(R_NM), nmbc, [t_nm])

                prev_y = []
                hid_b = list(ev_prev)
                for j in range(NT):
                    xr = fr0
                    t_f = dma("gpsimd", "sFR0",
                              xr, fs[j * P:(j + 1) * P, :],
                              waits=fr_busy[0] + fr_busy[1] + prev_y)
                    t_x2 = dma("gpsimd", "sX2Y", hidf[:],
                               x2s[j * P:(j + 1) * P, fc * TB:(fc + 1) * TB],
                               waits=hid_b + prev_y)
                    t_y1 = vop(lambda e, xr=xr: e.tensor_mul(sqs[:], xr, sbc[:]),
                               waits=[t_f, t_sbc, t_nmbc] + prev_y)
                    t_y2 = vop(lambda e: e.tensor_add(sgs[:], sqs[:], nmbc[:]),
                               waits=[t_y1])
                    t_y3 = vop(lambda e: e.tensor_add(yt, sgs[:], hidf[:]),
                               waits=[t_y2, t_x2] + prev_y)
                    t_out = dma("gpsimd", "sO",
                                y_fm[j * P:(j + 1) * P, fc * TB:(fc + 1) * TB],
                                yt, waits=[t_y3])
                    fr_busy[0] = [t_y1]
                    fr_busy[1] = [t_y1]
                    hid_b = [t_y3]
                    prev_y = [t_y3, t_out]

            n_out = em.cur("sO")
            em.emit("gpsimd", lambda e: e.nop(), waits=[("sO", n_out)])

        with ExitStack() as sem_ctx:
            sems = {s: sem_ctx.enter_context(nc.semaphore(s)) for s in SEM_NAMES}
            block = sem_ctx.enter_context(nc.Block())

            def run(eng_name, eng):
                for waits, fn, inc in em.ops[eng_name]:
                    for s, v in waits:
                        eng.wait_ge(sems[s], v)
                    inst = fn(eng)
                    if inc is not None:
                        s, amt = inc
                        inst.then_inc(sems[s], amt)

            @block.sync
            def _(sync):
                run("sync", sync)

            @block.tensor
            def _(tensor):
                run("tensor", tensor)

            @block.vector
            def _(vector):
                run("vector", vector)

            @block.scalar
            def _(scalar):
                run("scalar", scalar)

            @block.gpsimd
            def _(gpsimd):
                run("gpsimd", gpsimd)

    return nc


_NC_CACHE = None


def _get_nc():
    global _NC_CACHE
    if _NC_CACHE is None:
        _NC_CACHE = build_nc()
    return _NC_CACHE


def _tile_lhsT(w, nk, nm):
    t = w.T.reshape(nk, P, nm, P).transpose(2, 1, 0, 3).reshape(nm * P, nk * P)
    return np.ascontiguousarray(t)


def kernel(L0_attn_hidden_states, L0_attn_Wi, L0_attn_Wf, L0_attn_Wg,
           L0_attn_Wo, L0_attn_gnorm_w, L0_ffn_W1, L0_ffn_W2, L0_ffn_W3):
    import ml_dtypes
    x = np.ascontiguousarray(np.asarray(L0_attn_hidden_states, dtype=np.float32))
    wi = _tile_lhsT(np.asarray(L0_attn_Wi, dtype=np.float32), NK, NT)
    wf = _tile_lhsT(np.asarray(L0_attn_Wf, dtype=np.float32), NK, NT)
    wg = _tile_lhsT(np.asarray(L0_attn_Wg, dtype=np.float32), NK, NT)
    wo = _tile_lhsT(np.asarray(L0_attn_Wo, dtype=np.float32), NK, NT)
    w1 = _tile_lhsT(np.asarray(L0_ffn_W1, dtype=np.float32), NK, NDH)
    w3 = _tile_lhsT(np.asarray(L0_ffn_W3, dtype=np.float32), NK, NDH)
    w2 = np.ascontiguousarray(
        np.asarray(L0_ffn_W2, dtype=np.float32).T.reshape(NDH, P, NG, TB)
        .transpose(2, 0, 1, 3).reshape(NG * NDH * P, TB))
    gnw = np.ascontiguousarray(
        np.asarray(L0_attn_gnorm_w, dtype=np.float32).reshape(NT, P).T)
    i = np.arange(P, dtype=np.uint32)
    anded = i[:, None] & i[None, :]
    pop = np.zeros((P, P), dtype=np.int64)
    for b in range(7):
        pop += (anded >> b) & 1
    h128 = np.where(pop % 2 == 0, 1.0, -1.0).astype(ml_dtypes.bfloat16)
    onescol = np.ones((P, 1), dtype=ml_dtypes.bfloat16)
    onesrow = np.ones((1, P), dtype=np.float32)

    nc = _get_nc()
    in_maps = []
    for c in range(8):
        b, half = c // 2, c % 2
        xm = np.ascontiguousarray(x[b, half * TCORE:(half + 1) * TCORE, :].T)
        xc = np.ascontiguousarray(x[b, 0:TCORE, :].T)
        mask = np.full((P, 1), float(half), dtype=np.float32)
        in_maps.append({
            "x_fm": xm, "xc_fm": xc, "wi": wi, "wf": wf, "wg": wg, "wo": wo,
            "w1": w1, "w3": w3, "w2": w2, "gnw": gnw, "h128": h128,
            "onescol": onescol, "onesrow": onesrow, "maskcol": mask,
        })
    res = run_bass_kernel_spmd(nc, in_maps, core_ids=list(range(8)))
    out = np.empty_like(x)
    for c in range(8):
        b, half = c // 2, c % 2
        out[b, half * TCORE:(half + 1) * TCORE, :] = res.results[c]["y_fm"].T
    return out


# version 22 (final)
# speedup vs baseline: 1.0630x; 1.0070x over previous
import numpy as np
from contextlib import ExitStack
import concourse.bass as bass
import concourse.mybir as mybir
from concourse.bass_utils import run_bass_kernel_spmd

F32 = mybir.dt.float32
BF16 = mybir.dt.bfloat16
ALU = mybir.AluOpType
AF = mybir.ActivationFunctionType

P = 128
D = 2048
HID = 8192
TCORE = 1024
TB = 512
NCH = TCORE // TB
NT = D // P
NK = D // P
NDH = HID // P
NG = 4
INV_D = 1.0 / D
EPS = 1e-5
SC_H = float(1.0 / np.sqrt(HID))

ENGINES = ("sync", "tensor", "vector", "scalar", "gpsimd")
DMA_SEMS = {"sC", "sWH0", "sWH1", "sWH2", "sWL0", "sWL1", "sWL2",
            "sCV0", "sCV1", "sCVH0", "sCVH1", "sCVL0", "sCVL1",
            "sL", "sXR0", "sXR1", "sSP", "sRL0", "sRL1",
            "sF1", "sFS", "sFR0", "sFR1", "sX2Y", "sO"}
SEM_NAMES = tuple(sorted(DMA_SEMS)) + ("s_t", "s_v", "s_a", "s_g")


class Em:

    def __init__(self):
        self.ops = {e: [] for e in ENGINES}
        self.cnt = {s: 0 for s in SEM_NAMES}
        self.own = {e: {s: 0 for s in SEM_NAMES} for e in ENGINES}
        self.waited = {e: {s: 0 for s in SEM_NAMES} for e in ENGINES}

    def emit(self, eng, fn, waits=(), inc=None):
        best = {}
        for s, v in waits:
            if v > best.get(s, 0):
                best[s] = v
        real = []
        for s, v in best.items():
            if self.waited[eng][s] >= v:
                continue
            real.append((s, v))
            self.waited[eng][s] = v
        tok = None
        if inc is not None:
            s, amt = inc
            self.cnt[s] += amt
            self.own[eng][s] += amt
            tok = (s, self.cnt[s])
        self.ops[eng].append((tuple(real), fn, inc))
        return tok

    def cur(self, sem):
        return self.cnt[sem]

    def fence(self):
        return [(s, self.cnt[s]) for s in SEM_NAMES]


class Scr:

    def __init__(self, tshi, tslo, tshi2, tslo2, rowf, sqq):
        self.tshi, self.tslo = tshi, tslo
        self.tshi2, self.tslo2 = tshi2, tslo2
        self.rowf = rowf
        self.sqq = sqq

R_M, R_S, R_NM, R_T0 = range(4)


def build_nc():
    nc = bass.Bass("TRN2", target_bir_lowering=False, debug=False)

    def din(name, shape, dt=F32):
        return nc.dram_tensor(name, shape, dt, kind="ExternalInput").ap()

    x_fm = din("x_fm", [D, TCORE])
    xc_fm = din("xc_fm", [D, TCORE])
    wi = din("wi", [D, D])
    wf = din("wf", [D, D])
    wg = din("wg", [D, D])
    wo = din("wo", [D, D])
    w1 = din("w1", [HID, D])
    w3 = din("w3", [HID, D])
    w2 = din("w2", [NG * NDH * P, TB])
    gnw = din("gnw", [P, NT])
    h128 = din("h128", [P, P], BF16)
    onescol = din("onescol", [P, 1], BF16)
    onesrow = din("onesrow", [1, P])
    maskcol = din("maskcol", [P, 1])
    wpair_dram = {}
    for nm, ap_, rows in (("wi", wi, D), ("wf", wf, D), ("wg", wg, D),
                          ("wo", wo, D), ("w1", w1, HID), ("w3", w3, HID),
                          ("w2", w2, NG * NDH * P)):
        cols = TB if nm == "w2" else D
        hi_t = nc.dram_tensor(nm + "_h", [rows, cols], BF16, kind="Internal").ap()
        lo_t = nc.dram_tensor(nm + "_l", [rows, cols], BF16, kind="Internal").ap()
        wpair_dram[nm] = (ap_, hi_t, lo_t, rows, cols)
    y_fm = nc.dram_tensor("y_fm", [D, TCORE], F32, kind="ExternalOutput").ap()
    x2s = nc.dram_tensor("x2s", [D, TCORE], F32, kind="Internal").ap()
    s3row = nc.dram_tensor("s3row", [1, TCORE], F32, kind="Internal").ap()
    nm3row = nc.dram_tensor("nm3row", [1, TCORE], F32, kind="Internal").ap()
    fs = nc.dram_tensor("fs", [D, TB], F32, kind="Internal").ap()

    em = Em()

    with ExitStack() as ctx:
        def sb(name, shape, dt=F32):
            return ctx.enter_context(nc.sbuf_tensor(name, shape, dt))

        def psum(name, shape):
            return ctx.enter_context(nc.psum_tensor(name, shape, F32))

        c_h128 = sb("c_h128", [P, P], BF16)
        c_onescol = sb("c_onescol", [P, 1], BF16)
        c_onesrow = sb("c_onesrow", [1, P])
        c_maskcol = sb("c_maskcol", [P, 1])
        c_gnw = sb("c_gnw", [P, NT])
        cstate = sb("cstate", [P, NT])
        cstate_c = sb("cstate_c", [P, NT])
        whi = [sb(f"whi{i}", [P, 2048], BF16) for i in range(2)] + [None]
        wlo = [sb(f"wlo{i}", [P, 2048], BF16) for i in range(2)] + [None]
        wnslots = [2]
        sqs = sb("sqs", [P, TB])
        sgs = sb("sgs", [P, TB])
        sbc = sb("sbc", [P, TB])
        nmbc = sb("nmbc", [P, TB])
        g_rows = sb("g_rows", [1, 4 * TB])

        def g_rowf(i):
            return g_rows[0:1, i * TB:(i + 1) * TB]

        pA = psum("pA", [P, TB])
        pB = psum("pB", [P, TB])
        pC = psum("pC", [P, TB])
        pD = psum("pD", [P, TB])
        pE = psum("pE", [P, TB])
        pF = psum("pF", [P, TB])
        psum_free = {id(p): [] for p in (pA, pB, pC, pD, pE, pF)}

        def dma(eng, sem, out_ap, in_ap, waits=()):
            selfw = [(sem, em.cur(sem))]
            return em.emit(eng, lambda e, o=out_ap, i=in_ap: e.dma_start(out=o, in_=i),
                           waits=list(waits) + selfw, inc=(sem, 16))

        def vop(fn, waits=()):
            return em.emit("vector", fn, waits=waits, inc=("s_v", 1))

        def aop(fn, waits=()):
            return em.emit("scalar", fn, waits=waits, inc=("s_a", 1))

        def gop(fn, waits=()):
            return em.emit("gpsimd", fn, waits=waits, inc=("s_g", 1))

        def top(fn, waits=(), inc=True):
            return em.emit("tensor", fn, waits=waits,
                           inc=(("s_t", 1) if inc else None))

        for dst, srcap in ((c_h128, h128), (c_onescol, onescol),
                           (c_onesrow, onesrow), (c_maskcol, maskcol),
                           (c_gnw, gnw)):
            dma("gpsimd", "sC", dst[:], srcap[:])
        t_memset = gop(lambda e: e.memset(cstate_c[:], 0.0))
        statics_w = [("sC", em.cur("sC")), t_memset]

        cw_holder, cwh_holder, cwl_holder = [], [], []

        class _H:
            def __init__(self, h, i):
                self.h = h
                self.i = i
            def __getitem__(self, k):
                return self.h[self.i][k]

        cw = [_H(cw_holder, 0), _H(cw_holder, 1)]
        cwh = [_H(cwh_holder, 0), _H(cwh_holder, 1)]
        cwl = [_H(cwl_holder, 0), _H(cwl_holder, 1)]

        conv_order = ("wi", "wf", "wg", "wo", "w1", "w3", "w2")
        conv_queue = []
        for nm in conv_order:
            _, _, _, rows, cols = wpair_dram[nm]
            for si in range(rows // P):
                for col0 in range(0, cols, 1024):
                    conv_queue.append((nm, si, col0, min(1024, cols - col0)))
        conv_tok = {}
        conv_part = {}
        conv_next = [0]
        conv_busy = [{"cw": [], "cwh": [], "cwl": []} for _ in range(2)]

        conv_in_tok = {}
        conv_in_next = [0]

        def _conv_emit_in(k):
            if k >= len(conv_queue) or k in conv_in_tok:
                return
            nm, si, col0, nc_ = conv_queue[k]
            p = k % 2
            src_w, _, _, rows, cols = wpair_dram[nm]
            rs = slice(si * P, (si + 1) * P)
            cs = slice(col0, col0 + nc_)
            conv_in_tok[k] = dma("gpsimd", "sCV0" if p == 0 else "sCV1",
                                 cw[p][:, 0:nc_], src_w[rs, cs],
                                 waits=conv_busy[p]["cw"])

        def conv_pump(n=1):
            for _ in range(n):
                k = conv_next[0]
                if k >= len(conv_queue):
                    return
                conv_next[0] += 1
                _conv_emit_in(k)
                _conv_emit_in(k + 1)
                nm, si, col0, nc_ = conv_queue[k]
                p = k % 2
                _, hi_t, lo_t, rows, cols = wpair_dram[nm]
                rs = slice(si * P, (si + 1) * P)
                cs = slice(col0, col0 + nc_)
                bz = conv_busy[p]
                t_in = conv_in_tok.pop(k)
                t_hi = aop(lambda e, p=p, c=nc_: e.copy(out=cwh[p][:, 0:c],
                           in_=cw[p][:, 0:c]),
                           waits=[t_in] + bz["cwh"])
                t_lo = gop(lambda e, p=p, c=nc_: e.tensor_sub(
                    cwl[p][:, 0:c], cw[p][:, 0:c], cwh[p][:, 0:c]),
                    waits=[t_in, t_hi] + bz["cwl"])
                t_oh = dma("gpsimd", "sCVH0" if p == 0 else "sCVH1",
                           hi_t[rs, cs], cwh[p][:, 0:nc_], waits=[t_hi])
                t_ol = dma("gpsimd", "sCVL0" if p == 0 else "sCVL1",
                           lo_t[rs, cs], cwl[p][:, 0:nc_], waits=[t_lo])
                bz["cw"] = [t_hi, t_lo]
                bz["cwh"] = [t_oh]
                bz["cwl"] = [t_ol]
                key = (nm, si)
                conv_part.setdefault(key, []).extend([t_oh, t_ol])
                if col0 + nc_ >= cols:
                    conv_tok[key] = conv_part.pop(key)

        def conv_require(nm, si):
            while (nm, si) not in conv_tok:
                conv_pump(1)
            return conv_tok[(nm, si)]

        wslot_busy = [[], [], []]
        wcounter = [0]

        def wstream(nm, si):
            _, hi_t, lo_t, rows, cols = wpair_dram[nm]
            cv = conv_require(nm, si)
            i = wcounter[0]
            wcounter[0] += 1
            slot = i % wnslots[0]
            rs = slice(si * P, (si + 1) * P)
            t_h = dma("sync", ("sWH0", "sWH1", "sWH2")[slot],
                      whi[slot][:, 0:cols], hi_t[rs, :],
                      waits=cv + wslot_busy[slot])
            t_l = dma("sync", ("sWL0", "sWL1", "sWL2")[slot],
                      wlo[slot][:, 0:cols], lo_t[rs, :],
                      waits=cv + wslot_busy[slot])

            def done_cb(mm_tok, s=slot):
                wslot_busy[s] = [mm_tok]
            return slot, [t_h, t_l], done_cb

        def mm3_group(bank, nk, lhsT_hi_f, lhsT_lo_f, rhs_hi_f, rhs_lo_f, waits,
                      rhs_waits_f=None):
            pf = psum_free[id(bank)]
            tok = None
            for k in range(nk):
                w0 = (list(waits) + pf) if k == 0 else []
                if rhs_waits_f is not None:
                    w0 = w0 + list(rhs_waits_f(k))
                first = (k == 0)
                last = (k == nk - 1)
                top(lambda e, a=lhsT_hi_f(k), b=rhs_hi_f(k), s=first: e.matmul(
                    bank[:], a, b, start=s, stop=False), waits=w0, inc=False)
                top(lambda e, a=lhsT_hi_f(k), b=rhs_lo_f(k): e.matmul(
                    bank[:], a, b, start=False, stop=False), inc=False)
                tok = top(lambda e, a=lhsT_lo_f(k), b=rhs_hi_f(k), s=last: e.matmul(
                    bank[:], a, b, start=False, stop=s))
            return tok

        def psum_release(bank, tok):
            psum_free[id(bank)] = [tok]

        stats_busy = [[]]
        rows_busy = [[]]

        class StatsAcc:
            def __init__(self, scr, ntiles, want_sum=True, single=False,
                         banks=None, busy=None):
                self.scr = scr
                self.n = ntiles
                self.want_sum = want_sum
                self.single = single
                self.bS, self.bQ = banks if banks is not None else (pE, pF)
                self.busy = busy if busy is not None else stats_busy
                self.j = 0
                self.prev_sum = [list(self.busy[0]), list(self.busy[0])]
                self.prev_sq = [list(self.busy[0]), list(self.busy[0])]
                self.tok_sum = None
                self.tok_sq = None

            def tile(self, t_ap, waits):
                scr = self.scr
                bS, bQ = self.bS, self.bQ
                j, n = self.j, self.n
                self.j += 1
                p = 0 if self.single else j % 2
                jmin = 1 if self.single else 2
                if self.want_sum:
                    ps = self.prev_sum[p] if j >= jmin else list(self.busy[0])
                    t_hi = aop(lambda e, t=t_ap, p=p: e.copy(out=scr.tshi[p][:],
                               in_=t), waits=list(waits) + ps)
                    t_lo = vop(lambda e, t=t_ap, p=p: e.tensor_sub(scr.tslo[p][:],
                               t, scr.tshi[p][:]),
                               waits=list(waits) + [t_hi] + ps)
                    w0 = [t_hi, t_lo] + (psum_free[id(bS)] if j == 0 else [])
                    top(lambda e, s=(j == 0), p=p: e.matmul(bS[0:1, :],
                        c_onescol[:], scr.tshi[p][:], start=s, stop=False),
                        waits=w0, inc=False)
                    self.tok_sum = top(lambda e, s=(j == n - 1), p=p: e.matmul(
                        bS[0:1, :], c_onescol[:], scr.tslo[p][:], start=False,
                        stop=s))
                    self.prev_sum[p] = [self.tok_sum]
                pq = self.prev_sq[p] if j >= jmin else list(self.busy[0])
                t_sq = vop(lambda e, t=t_ap, p=p: e.tensor_mul(scr.sqq[p][:], t, t),
                           waits=list(waits) + pq)
                t_hi2 = aop(lambda e, p=p: e.copy(out=scr.tshi2[p][:], in_=scr.sqq[p][:]),
                            waits=[t_sq] + pq)
                t_lo2 = vop(lambda e, p=p: e.tensor_sub(scr.tslo2[p][:], scr.sqq[p][:],
                            scr.tshi2[p][:]),
                            waits=[t_sq, t_hi2] + pq)
                w0 = [t_hi2, t_lo2] + (psum_free[id(bQ)] if j == 0 else [])
                top(lambda e, s=(j == 0), p=p: e.matmul(bQ[0:1, :], c_onescol[:],
                    scr.tshi2[p][:], start=s, stop=False), waits=w0, inc=False)
                self.tok_sq = top(lambda e, s=(j == n - 1), p=p: e.matmul(
                    bQ[0:1, :], c_onescol[:], scr.tslo2[p][:], start=False, stop=s))
                self.prev_sq[p] = [self.tok_sq]
                return ([self.tok_sum] if self.want_sum else []) + [self.tok_sq]

            def finish(self):
                self.busy[0] = (([self.tok_sum] if self.tok_sum else [])
                                + [self.tok_sq])
                return self.tok_sum, self.tok_sq

        def ln_rows(scr, tok_sum, tok_sq, nfeat=D, banks=None):
            bS, bQ = banks if banks is not None else (pE, pF)
            rb = rows_busy[0]
            r = g_rowf
            inv = 1.0 / nfeat
            t_m = aop(lambda e: e.copy(out=r(R_M), in_=bS[0:1, :]),
                      waits=[tok_sum] + rb)
            t_m2 = aop(lambda e: e.mul(out=r(R_M), in_=r(R_M), mul=inv),
                       waits=[t_m])
            t_q = aop(lambda e: e.copy(out=r(R_T0), in_=bQ[0:1, :]),
                      waits=[tok_sq] + rb)
            t_msq = vop(lambda e: e.tensor_mul(r(R_NM), r(R_M), r(R_M)),
                        waits=[t_m2] + rb)
            t_v = vop(lambda e: e.scalar_tensor_tensor(r(R_T0), r(R_T0), inv,
                      r(R_NM), ALU.mult, ALU.subtract), waits=[t_q, t_msq] + rb)
            t_ve = vop(lambda e: e.tensor_scalar_add(r(R_T0), r(R_T0), EPS),
                       waits=[t_v])
            t_rc = vop(lambda e: e.reciprocal(r(R_NM), r(R_T0)),
                       waits=[t_ve, t_msq] + rb)
            t_s = aop(lambda e: e.activation(out=r(R_S), in_=r(R_NM),
                      func=AF.Sqrt), waits=[t_rc] + rb)
            t_nm = vop(lambda e: e.scalar_tensor_tensor(r(R_NM), r(R_M), -1.0,
                       r(R_S), ALU.mult, ALU.mult), waits=[t_m2, t_s, t_rc] + rb)
            psum_release(bS, t_m)
            psum_release(bQ, t_q)
            rows_busy[0] = [t_s, t_nm]
            return t_s, t_nm

        bc_busy = [[]]

        def bcast(row_ap, dst, waits):
            pf = psum_free[id(pD)]
            t_mm = top(lambda e, r=row_ap: e.matmul(pD[:], c_onesrow[:], r,
                                                    start=True, stop=True),
                       waits=list(waits) + pf + statics_w)
            t_ev = vop(lambda e, d=dst: e.tensor_copy(d[:], pD[:]),
                       waits=[t_mm] + bc_busy[0])
            psum_release(pD, t_ev)
            rows_busy[0] = rows_busy[0] + [t_mm]
            return t_ev

        def tcol(buf, j, width=TB):
            return buf[:, j * width:(j + 1) * width]

        with ExitStack() as attn_ctx:
            def asb(name, shape, dt=F32):
                return attn_ctx.enter_context(nc.sbuf_tensor(name, shape, dt))
            xbuf = asb("xbuf", [P, NT * TB])
            vbuf = asb("vbuf", [P, NT * TB])
            lnhi = asb("lnhi", [P, NT * TB], BF16)
            lnlo = asb("lnlo", [P, NT * TB], BF16)
            oghi = asb("oghi", [P, NT * TB], BF16)
            oglo = asb("oglo", [P, NT * TB], BF16)
            xr0 = asb("xr0", [P, TB])
            xr1 = asb("xr1", [P, TB])
            x2t = asb("x2t", [P, TB])
            rbc = asb("rbc", [P, TB])
            nmbc2 = asb("nmbc2", [P, TB])
            b_tshi = asb("b_tshi", [P, TB], BF16)
            b_tslo = asb("b_tslo", [P, TB], BF16)
            b_tshi2 = asb("b_tshi2", [P, TB], BF16)
            b_tslo2 = asb("b_tslo2", [P, TB], BF16)
            b_sqq = asb("b_sqq", [P, TB])
            a_tshi = [asb(f"a_tshi{i}", [P, TB], BF16) for i in range(2)]
            a_tslo = [asb(f"a_tslo{i}", [P, TB], BF16) for i in range(2)]
            a_tshi2 = [asb(f"a_tshi2{i}", [P, TB], BF16) for i in range(2)]
            a_tslo2 = [asb(f"a_tslo2{i}", [P, TB], BF16) for i in range(2)]
            a_sqq = [asb(f"a_sqq{i}", [P, TB]) for i in range(2)]
            for i in range(2):
                cw_holder.append(asb(f"cw{i}", [P, 1024]))
                cwh_holder.append(asb(f"cwh{i}", [P, 1024], BF16))
                cwl_holder.append(asb(f"cwl{i}", [P, 1024], BF16))
            conv_pump(16)
            scrA = Scr(a_tshi, a_tslo, a_tshi2, a_tslo2, g_rowf, a_sqq)
            scrB = Scr([b_tshi], [b_tslo], [b_tshi2], [b_tslo2], g_rowf,
                       [b_sqq])

            def ln1_load(src_dram, mc, xdst, ld_waits):
                src_chunk = src_dram[:, mc * TB:(mc + 1) * TB].rearrange(
                    "(j p) c -> p j c", p=P)
                dst_chunk = xdst[:, 0:NT * TB].rearrange("p (j c) -> p j c", j=NT)
                return dma("gpsimd", "sL", dst_chunk, src_chunk,
                           waits=list(ld_waits))

            def ln1_apply(xdst, hi_buf, lo_buf, t_ld, t_sbc, t_nmbc,
                          s_tile, nm_tile, sqs_prev=None, set_bc=True):
                pair_ready = []
                prev_t3 = []
                prev_t2 = [] if sqs_prev is None else list(sqs_prev)
                prev_t4 = []
                for j in range(NT):
                    t1 = vop(lambda e, j=j, xd=xdst, st=s_tile: e.tensor_mul(
                             sqs[:], tcol(xd, j), st[:]),
                             waits=[t_ld, t_nmbc, t_sbc] + prev_t2)
                    t2 = vop(lambda e, nt=nm_tile: e.tensor_add(
                             sgs[:], sqs[:], nt[:]),
                             waits=[t1] + prev_t3 + prev_t4)
                    t3 = aop(lambda e, j=j: e.copy(out=tcol(hi_buf, j), in_=sgs[:]),
                             waits=[t2])
                    t4 = gop(lambda e, j=j: e.tensor_sub(tcol(lo_buf, j), sgs[:],
                                                         tcol(hi_buf, j)),
                             waits=[t2, t3])
                    prev_t3 = [t3]
                    prev_t2 = [t2]
                    prev_t4 = [t4]
                    pair_ready.append([t3, t4])
                if set_bc:
                    bc_busy[0] = prev_t2
                return pair_ready, prev_t2

            def load_and_ln1(src_dram, mc, hi_buf=None, lo_buf=None):
                if hi_buf is None:
                    hi_buf, lo_buf = lnhi, lnlo
                t_ld = ln1_load(src_dram, mc, xbuf, em.fence())
                acc = StatsAcc(scrA, NT)
                for j in range(NT):
                    acc.tile(tcol(xbuf, j), [t_ld])
                tok_sum, tok_sq = acc.finish()
                t_s, t_nm = ln_rows(scrA, tok_sum, tok_sq)
                t_sbc = bcast(g_rowf(R_S), sbc, [t_s])
                t_nmbc = bcast(g_rowf(R_NM), nmbc, [t_nm])
                pair_ready, _ = ln1_apply(xbuf, hi_buf, lo_buf, t_ld,
                                          t_sbc, t_nmbc, sbc, nmbc)
                return pair_ready

            def proj_weight(wsrc, dt, pair_ready, bank):
                conv_pump(4)
                slot, ready, done_cb = wstream(wsrc, dt)
                tok = mm3_group(
                    bank, NK,
                    lambda k, s=slot: whi[s][:, k * P:(k + 1) * P],
                    lambda k, s=slot: wlo[s][:, k * P:(k + 1) * P],
                    lambda k: tcol(lnhi, k),
                    lambda k: tcol(lnlo, k),
                    ready,
                    rhs_waits_f=lambda k: pair_ready[k])
                done_cb(tok)
                return tok

            carry_tok_c = [[] for _ in range(NT)]
            carry_tok = [[] for _ in range(NT)]
            pr0 = load_and_ln1(xc_fm, 1)
            a_ld = ln1_load(x_fm, 0, xbuf, [t for pr in pr0 for t in pr])
            a_acc = StatsAcc(scrA, NT)
            a_jj = [0]
            a_bc = [None, None]
            sc_prev = [t_memset]
            vb_busy = [[] for _ in range(4)]
            for dt in range(NT):
                conv_pump(6)
                slot, ready, done_i = wstream("wi", dt)
                tok_i0 = mm3_group(
                    pA, NK,
                    lambda k, s=slot: whi[s][:, k * P:(k + 1) * P],
                    lambda k, s=slot: wlo[s][:, k * P:(k + 1) * P],
                    lambda k: tcol(lnhi, k), lambda k: tcol(lnlo, k),
                    ready, rhs_waits_f=lambda k: pr0[k])
                done_i(tok_i0)
                t_si0 = aop(lambda e: e.activation(out=tcol(vbuf, 0), in_=pA[:],
                            func=AF.Silu), waits=[tok_i0] + vb_busy[0])
                psum_release(pA, t_si0)
                slot, ready, done_f = wstream("wf", dt)
                tok_f0 = mm3_group(
                    pC, NK,
                    lambda k, s=slot: whi[s][:, k * P:(k + 1) * P],
                    lambda k, s=slot: wlo[s][:, k * P:(k + 1) * P],
                    lambda k: tcol(lnhi, k), lambda k: tcol(lnlo, k),
                    ready, rhs_waits_f=lambda k: pr0[k])
                done_f(tok_f0)
                t_gt0 = aop(lambda e: e.activation(out=tcol(vbuf, 2), in_=pC[:],
                            func=AF.Sigmoid), waits=[tok_f0] + vb_busy[2])
                psum_release(pC, t_gt0)
                t_v1 = vop(lambda e: e.tensor_mul(sqs[:], tcol(vbuf, 0),
                           tcol(vbuf, 2)), waits=[t_si0, t_gt0])
                t_v2 = vop(lambda e: e.tensor_sub(tcol(vbuf, 0), tcol(vbuf, 0),
                           sqs[:]), waits=[t_v1])
                t_sc0 = vop(lambda e, dt=dt: e.tensor_tensor_scan(
                    tcol(vbuf, 0), tcol(vbuf, 2), tcol(vbuf, 0),
                    cstate_c[:, dt:dt + 1], ALU.mult, ALU.add),
                    waits=[t_v2, t_memset] + sc_prev)
                t_cs0 = gop(lambda e, dt=dt: e.tensor_copy(
                    cstate_c[:, dt:dt + 1], tcol(vbuf, 0)[:, TB - 1:TB]),
                    waits=[t_sc0])
                carry_tok_c[dt] = [t_cs0]
                sc_prev = [t_sc0, t_cs0]
                vb_busy[0] = [t_sc0, t_cs0]
                vb_busy[2] = [t_sc0]
                if dt >= 2:
                    for _ in range(2):
                        if a_jj[0] < NT:
                            a_acc.tile(tcol(xbuf, a_jj[0]), [a_ld])
                            a_jj[0] += 1
                if dt == 10:
                    a_sum, a_sq = a_acc.finish()
                    a_s, a_nm = ln_rows(scrA, a_sum, a_sq)
                    a_bc[0] = bcast(g_rowf(R_S), sbc, [a_s])
                    a_bc[1] = bcast(g_rowf(R_NM), nmbc, [a_nm])

            pr_cur, _ = ln1_apply(xbuf, lnhi, lnlo, a_ld, a_bc[0], a_bc[1],
                                  sbc, nmbc, sqs_prev=sc_prev)

            t_init = vop(lambda e: e.tensor_scalar(
                cstate[:], cstate_c[:], c_maskcol[:], None, ALU.mult),
                waits=em.fence())

            x2_t1_prev = [[] for _ in range(NT)]
            for mc in range(NCH):
                pair_ready = pr_cur
                si_toks, gt_toks = [], []
                for dt in range(NT):
                    bank = pA if dt % 2 == 0 else pB
                    tok_i = proj_weight("wi", dt, pair_ready, bank)
                    t_si = aop(lambda e, dt=dt, b=bank: e.activation(
                        out=tcol(vbuf, dt), in_=b[:], func=AF.Silu), waits=[tok_i])
                    psum_release(bank, t_si)
                    si_toks.append(t_si)
                for dt in range(NT):
                    bank = pA if dt % 2 == 0 else pB
                    tok_f = proj_weight("wf", dt, pair_ready, bank)
                    t_gt = aop(lambda e, dt=dt, b=bank: e.activation(
                        out=tcol(xbuf, dt), in_=b[:], func=AF.Sigmoid),
                        waits=[tok_f] + x2_t1_prev[dt])
                    psum_release(bank, t_gt)
                    gt_toks.append(t_gt)
                scan_prev = [t_init]
                o_done = []
                for dt in range(NT):
                    t_v1 = vop(lambda e, dt=dt: e.tensor_mul(
                        sqs[:], tcol(vbuf, dt), tcol(xbuf, dt)),
                        waits=[si_toks[dt], gt_toks[dt]])
                    t_v2 = vop(lambda e, dt=dt: e.tensor_sub(
                        tcol(vbuf, dt), tcol(vbuf, dt), sqs[:]), waits=[t_v1])
                    t_sc = vop(lambda e, dt=dt: e.tensor_tensor_scan(
                        tcol(vbuf, dt), tcol(xbuf, dt), tcol(vbuf, dt),
                        cstate[:, dt:dt + 1], ALU.mult, ALU.add),
                        waits=[t_v2] + scan_prev + carry_tok[dt])
                    t_cs = gop(lambda e, dt=dt: e.tensor_copy(
                        cstate[:, dt:dt + 1], tcol(vbuf, dt)[:, TB - 1:TB]),
                        waits=[t_sc])
                    carry_tok[dt] = [t_cs]
                    scan_prev = [t_sc, t_cs]
                    o_done.append([t_sc, t_cs])

                sg_toks = []
                for dt in range(NT):
                    bank = pA if dt % 2 == 0 else pB
                    tok_g = proj_weight("wg", dt, pair_ready, bank)
                    t_sg = aop(lambda e, dt=dt, b=bank: e.activation(
                        out=tcol(xbuf, dt), in_=b[:], func=AF.Silu),
                        waits=[tok_g] + o_done[dt])
                    psum_release(bank, t_sg)
                    sg_toks.append(t_sg)

                acc = StatsAcc(scrA, NT, want_sum=False)
                for j in range(NT):
                    acc.tile(tcol(vbuf, j), o_done[j])
                _, tok_sq = acc.finish()
                t_rq = aop(lambda e: e.activation(out=g_rowf(R_T0),
                           in_=pF[0:1, :], func=AF.Copy, bias=EPS, scale=INV_D),
                           waits=[tok_sq] + rows_busy[0])
                psum_release(pF, t_rq)
                t_rc = vop(lambda e: e.reciprocal(g_rowf(R_NM),
                           g_rowf(R_T0)), waits=[t_rq] + rows_busy[0])
                t_rms = aop(lambda e: e.activation(out=g_rowf(R_S),
                            in_=g_rowf(R_NM), func=AF.Sqrt),
                            waits=[t_rc] + rows_busy[0])
                rows_busy[0] = [t_rms, t_rc]
                t_rbc = bcast(g_rowf(R_S), rbc, [t_rms])

                og_pr = []
                prev_o3 = []
                for dt in range(NT):
                    t_o2 = vop(lambda e, dt=dt: e.tensor_scalar(
                        sqs[:], tcol(vbuf, dt), c_gnw[:, dt:dt + 1], None,
                        ALU.mult), waits=o_done[dt] + prev_o3)
                    t_o3 = vop(lambda e, dt=dt: e.tensor_mul(
                        tcol(vbuf, dt), sqs[:], tcol(xbuf, dt)),
                        waits=[t_o2, sg_toks[dt]])
                    t_h = aop(lambda e, dt=dt: e.copy(out=tcol(oghi, dt),
                              in_=tcol(vbuf, dt)), waits=[t_o3])
                    t_l = gop(lambda e, dt=dt: e.tensor_sub(
                        tcol(oglo, dt), tcol(vbuf, dt), tcol(oghi, dt)),
                        waits=[t_o3, t_h])
                    prev_o3 = [t_o3]
                    og_pr.append([t_h, t_l])

                if mc + 1 < NCH:
                    b_ld = ln1_load(x_fm, mc + 1, vbuf,
                                    [t for pr in og_pr for t in pr])
                    b_busy = [[]]
                    b_acc = StatsAcc(scrB, NT, single=True, banks=(pC, pD),
                                     busy=b_busy)
                    b_jj = [0]
                    b_rows = [None]

                acc2 = StatsAcc(scrA, NT)
                evs = []
                for ht in range(NT):
                    bank = pA if ht % 2 == 0 else pB
                    conv_pump(4)
                    slot, ready, done_cb = wstream("wo", ht)
                    tok = mm3_group(
                        bank, NK,
                        lambda k, s=slot: whi[s][:, k * P:(k + 1) * P],
                        lambda k, s=slot: wlo[s][:, k * P:(k + 1) * P],
                        lambda k: tcol(oghi, k),
                        lambda k: tcol(oglo, k),
                        ready,
                        rhs_waits_f=lambda k: og_pr[k])
                    done_cb(tok)
                    t_ev = vop(lambda e, ht=ht, b=bank: e.tensor_mul(
                        tcol(xbuf, ht), b[:], rbc[:]), waits=[tok, t_rbc])
                    psum_release(bank, t_ev)
                    evs.append(t_ev)
                    acc2.tile(tcol(xbuf, ht), [t_ev])
                    if mc + 1 < NCH:
                        if ht >= 2:
                            for _ in range(2):
                                if b_jj[0] < NT:
                                    b_acc.tile(tcol(vbuf, b_jj[0]), [b_ld])
                                    b_jj[0] += 1
                        if ht == 10:
                            b_sum, b_sq = b_acc.finish()
                            b_rows[0] = ln_rows(scrB, b_sum, b_sq,
                                                banks=(pC, pD))
                tok_sum, tok_sq = acc2.finish()
                if mc + 1 < NCH:
                    b_s, b_nm = b_rows[0]
                    b_sbc = bcast(g_rowf(R_S), rbc, [b_s] + evs)
                    b_nmbc = bcast(g_rowf(R_NM), nmbc2, [b_nm])
                t_s, t_nm = ln_rows(scrA, tok_sum, tok_sq)
                t_sbc = bcast(g_rowf(R_S), sbc, [t_s])
                t_nmbc = bcast(g_rowf(R_NM), nmbc, [t_nm])
                if mc + 1 < NCH:
                    pr_cur, b_last = ln1_apply(vbuf, lnhi, lnlo, b_ld,
                                               b_sbc, b_nmbc, rbc, nmbc2,
                                               sqs_prev=prev_o3, set_bc=False)
                else:
                    pr_cur, b_last = None, []

                acc3 = StatsAcc(scrA, NT)
                xr_busy = [[], []]
                prev_a = list(b_last)
                prev_sp = []
                x2_t1 = [[] for _ in range(NT)]
                for j in range(NT):
                    xr = xr0 if j % 2 == 0 else xr1
                    t_xr = dma("gpsimd", "sXR0" if j % 2 == 0 else "sXR1", xr[:],
                               x_fm[j * P:(j + 1) * P, mc * TB:(mc + 1) * TB],
                               waits=xr_busy[j % 2])
                    t_a1 = vop(lambda e, j=j: e.tensor_mul(sqs[:], tcol(xbuf, j),
                                                           sbc[:]),
                               waits=[t_sbc, t_nmbc] + prev_a)
                    x2_t1[j] = [t_a1]
                    t_a2 = vop(lambda e: e.tensor_add(sqs[:], sqs[:], nmbc[:]),
                               waits=[t_a1])
                    t_a3 = vop(lambda e, xr=xr: e.tensor_add(x2t[:], sqs[:], xr[:]),
                               waits=[t_a2, t_xr] + prev_sp)
                    t_sp = dma("gpsimd", "sSP",
                               x2s[j * P:(j + 1) * P, mc * TB:(mc + 1) * TB],
                               x2t[:], waits=[t_a3])
                    rd = acc3.tile(x2t[:], [t_a3])
                    xr_busy[j % 2] = [t_xr] + rd
                    prev_a = [t_a3]
                    prev_sp = [t_sp] + rd
                tok_sum, tok_sq = acc3.finish()
                t_s, t_nm = ln_rows(scrA, tok_sum, tok_sq)
                t_c1 = dma("gpsimd", "sSP", s3row[0:1, mc * TB:(mc + 1) * TB],
                           g_rowf(R_S), waits=[t_s])
                t_c2 = dma("gpsimd", "sSP", nm3row[0:1, mc * TB:(mc + 1) * TB],
                           g_rowf(R_NM), waits=[t_nm])
                rows_busy[0] = rows_busy[0] + [t_c1, t_c2]
                x2_t1_prev = x2_t1

            conv_pump(len(conv_queue))
        fence_attn = em.fence()
        with ExitStack() as ffn_ctx:
            def fsb(name, shape, dt=F32):
                return ffn_ctx.enter_context(nc.sbuf_tensor(name, shape, dt))
            zbuf = fsb("zbuf", [P, NDH * TB])
            whi[2] = fsb("whi2", [P, 2048], BF16)
            wlo[2] = fsb("wlo2", [P, 2048], BF16)
            wnslots[0] = 3
            wslot_busy[2] = list(fence_attn)
            freg = fsb("freg", [P, 2 * NT * TB], BF16)
            hidf = fsb("hidf", [P, TB])
            hidhi = fsb("hidhi", [P, TB], BF16)
            hidlo = fsb("hidlo", [P, TB], BF16)

            fr0 = fsb("fr0", [P, TB])[:]
            fr1 = fr0
            yt = sqs[:]

            flnhi = freg[:, 0:NT * TB]
            flnlo = freg[:, NT * TB:2 * NT * TB]
            fb16 = freg
            fb_base = 2 * (4 * TB + 6 * TB)

            def fb(i):
                return fb16[:, fb_base + i * TB: fb_base + (i + 1) * TB]

            f_tshi = [fb(0), fb(1)]
            f_tslo = [fb(2), fb(3)]
            f_tshi2 = [fb(4), fb(5)]
            f_tslo2 = [fb(6), fb(7)]
            zhi = fb(8)
            zlo = fb(9)
            scrF = Scr(f_tshi, f_tslo, f_tshi2, f_tslo2, g_rowf, [sgs, sgs])

            for fc in range(NCH):
                fw = fence_attn if fc == 0 else em.fence()
                t_r1 = dma("gpsimd", "sRL0", g_rowf(R_S),
                           s3row[0:1, fc * TB:(fc + 1) * TB],
                           waits=fw + rows_busy[0])
                t_r2 = dma("gpsimd", "sRL1", g_rowf(R_NM),
                           nm3row[0:1, fc * TB:(fc + 1) * TB],
                           waits=fw + rows_busy[0])
                rows_busy[0] = [t_r1, t_r2]
                t_sbc = bcast(g_rowf(R_S), sbc, fw + [t_r1])
                t_nmbc = bcast(g_rowf(R_NM), nmbc, fw + [t_r2])
                allpr = []
                hid_busy = [[], []]
                prev_t3 = []
                prev_t2 = []
                prev_t4 = []
                for j in range(NT):
                    t_ld = dma("gpsimd", "sF1", hidf[:],
                               x2s[j * P:(j + 1) * P, fc * TB:(fc + 1) * TB],
                               waits=fw + hid_busy[0])
                    t1 = vop(lambda e: e.tensor_mul(sqs[:], hidf[:], sbc[:]),
                             waits=[t_ld, t_sbc, t_nmbc] + prev_t2)
                    t2 = vop(lambda e: e.tensor_add(sgs[:], sqs[:], nmbc[:]),
                             waits=[t1] + prev_t3 + prev_t4)
                    t3 = aop(lambda e, j=j: e.copy(out=tcol(flnhi, j), in_=sgs[:]),
                             waits=[t2] + (fw if j == 0 else []))
                    t4 = gop(lambda e, j=j: e.tensor_sub(tcol(flnlo, j), sgs[:],
                                                         tcol(flnhi, j)),
                             waits=[t2, t3])
                    prev_t3 = [t3]
                    prev_t2 = [t2]
                    prev_t4 = [t4]
                    hid_busy[0] = [t1]
                    sqs_last = [t1, t2]
                    allpr.append([t3, t4])

                z_toks = [None] * NDH
                prev_hid = list(hid_busy[0])
                last_w13 = []
                phys = list(range(NDH))
                spare_phys = [NDH]

                def zap(pi):
                    return tcol(zbuf, pi) if pi < NDH else sqs[:]

                bf_tok = {i: [] for i in range(NDH)}
                bf_tok[NDH] = list(sqs_last)

                def bf_progress(j):
                    h = 1
                    while h < NDH and (j + 1) % (2 * h) == 0:
                        base = j + 1 - 2 * h
                        for i in range(base, base + h):
                            a, b = i, i + h
                            pa, pb_ = phys[a], phys[b]
                            sp = spare_phys[0]
                            w = bf_tok[pa] + bf_tok[pb_] + bf_tok[sp]
                            t_s_ = gop(lambda e, sp=sp, pa=pa, pb=pb_: e.tensor_add(
                                zap(sp), zap(pa), zap(pb)), waits=w)
                            t_d_ = gop(lambda e, pa=pa, pb=pb_: e.tensor_sub(
                                zap(pb), zap(pa), zap(pb)), waits=w)
                            phys[a], spare_phys[0] = sp, pa
                            bf_tok[phys[a]] = [t_s_]
                            bf_tok[pb_] = [t_d_]
                            bf_tok[spare_phys[0]] = [t_s_, t_d_]
                        h *= 2

                for dh in range(NDH):
                    slot1, ready1, done1 = wstream("w1", dh)
                    tok1 = mm3_group(
                        pA, NK,
                        lambda k, s=slot1: whi[s][:, k * P:(k + 1) * P],
                        lambda k, s=slot1: wlo[s][:, k * P:(k + 1) * P],
                        lambda k: tcol(flnhi, k),
                        lambda k: tcol(flnlo, k),
                        ready1,
                        rhs_waits_f=lambda k: allpr[k])
                    done1(tok1)
                    slot3, ready3, done3 = wstream("w3", dh)
                    tok3 = mm3_group(
                        pB, NK,
                        lambda k, s=slot3: whi[s][:, k * P:(k + 1) * P],
                        lambda k, s=slot3: wlo[s][:, k * P:(k + 1) * P],
                        lambda k: tcol(flnhi, k),
                        lambda k: tcol(flnlo, k),
                        ready3,
                        rhs_waits_f=lambda k: allpr[k])
                    done3(tok3)
                    t_su = aop(lambda e: e.activation(out=sgs[:], in_=pA[:],
                               func=AF.Silu), waits=[tok1] + prev_hid)
                    psum_release(pA, t_su)
                    t_hd = vop(lambda e: e.scalar_tensor_tensor(
                        hidf[:], pB[:], SC_H, sgs[:], ALU.mult, ALU.mult),
                        waits=[tok3, t_su] + prev_hid)
                    psum_release(pB, t_hd)
                    t_hh = aop(lambda e: e.copy(out=hidhi[:], in_=hidf[:]),
                               waits=[t_hd])
                    t_hl = vop(lambda e: e.tensor_sub(hidlo[:], hidf[:], hidhi[:]),
                               waits=[t_hd, t_hh])
                    pfC = psum_free[id(pC)]
                    top(lambda e: e.matmul(pC[:], c_h128[:], hidhi[:],
                        start=True, stop=False),
                        waits=[t_hh, t_hl] + pfC + statics_w, inc=False)
                    t_hm = top(lambda e: e.matmul(pC[:], c_h128[:], hidlo[:],
                               start=False, stop=True))
                    t_ze = aop(lambda e, dh=dh: e.copy(out=tcol(zbuf, dh), in_=pC[:]),
                               waits=[t_hm])
                    psum_release(pC, t_ze)
                    z_toks[dh] = t_ze
                    bf_tok[dh] = [t_ze]
                    bf_progress(dh)
                    prev_hid = [t_hm]
                    last_w13 = [tok1, tok3]

                lnpair_done = last_w13

                out_banks = [pA, pB, pC, pD]
                acc4 = StatsAcc(scrF, NT, single=True)
                ev_prev = prev_hid
                fr_busy = [list(lnpair_done), list(lnpair_done)]
                zpairs = [(zhi, zlo), (hidhi[:], hidlo[:])]
                zp_busy = [list(lnpair_done), list(lnpair_done) + prev_hid]
                for g in range(NG):
                    grp_last = [None] * 4
                    for hp in range(NDH):
                        slot, ready, done_cb = wstream("w2", g * NDH + hp)
                        pz = phys[hp]
                        zp = (g * NDH + hp) % 2
                        zh_, zl_ = zpairs[zp]
                        t_zh = aop(lambda e, pz=pz, zh=zh_: e.copy(
                            out=zh, in_=zap(pz)),
                            waits=bf_tok[pz] + zp_busy[zp])
                        t_zl = vop(lambda e, pz=pz, zh=zh_, zl=zl_: e.tensor_sub(
                            zl, zap(pz), zh),
                            waits=bf_tok[pz] + [t_zh] + zp_busy[zp])
                        for q in range(4):
                            bank = out_banks[q]
                            pf = psum_free[id(bank)] if hp == 0 else []
                            w0 = ready + [t_zh, t_zl] + pf
                            top(lambda e, b=bank, s=slot, q=q, zh=zh_,
                                st=(hp == 0): e.matmul(
                                b[:], whi[s][:, q * P:(q + 1) * P], zh,
                                start=st, stop=False), waits=w0, inc=False)
                            top(lambda e, b=bank, s=slot, q=q, zl=zl_: e.matmul(
                                b[:], whi[s][:, q * P:(q + 1) * P], zl,
                                start=False, stop=False), inc=False)
                            grp_last[q] = top(lambda e, b=bank, s=slot, q=q,
                                              zh=zh_, sp=(hp == NDH - 1): e.matmul(
                                b[:], wlo[s][:, q * P:(q + 1) * P], zh,
                                start=False, stop=sp))
                        done_cb(grp_last[3])
                        zp_busy[zp] = [grp_last[3]]
                    for q in range(4):
                        dt2 = g * 4 + q
                        xr = fr0
                        t_ev = vop(lambda e, b=out_banks[q]: e.tensor_copy(
                            hidf[:], b[:]), waits=[grp_last[q]] + ev_prev)
                        psum_release(out_banks[q], t_ev)
                        t_sp = dma("gpsimd", "sFS", fs[dt2 * P:(dt2 + 1) * P, :],
                                   hidf[:], waits=[t_ev])
                        t_ld = dma("gpsimd", "sFR0",
                                   xr, fs[dt2 * P:(dt2 + 1) * P, :],
                                   waits=[t_sp] + fr_busy[0] + fr_busy[1])
                        rd = acc4.tile(xr, [t_ld])
                        fr_busy[0] = rd
                        fr_busy[1] = rd
                        ev_prev = [t_sp]
                tok_sum, tok_sq = acc4.finish()
                t_s, t_nm = ln_rows(scrF, tok_sum, tok_sq)
                t_sbc = bcast(g_rowf(R_S), sbc, [t_s])
                t_nmbc = bcast(g_rowf(R_NM), nmbc, [t_nm])

                prev_y = []
                hid_b = list(ev_prev)
                for j in range(NT):
                    xr = fr0
                    t_f = dma("gpsimd", "sFR0",
                              xr, fs[j * P:(j + 1) * P, :],
                              waits=fr_busy[0] + fr_busy[1] + prev_y)
                    t_x2 = dma("gpsimd", "sX2Y", hidf[:],
                               x2s[j * P:(j + 1) * P, fc * TB:(fc + 1) * TB],
                               waits=hid_b + prev_y)
                    t_y1 = vop(lambda e, xr=xr: e.tensor_mul(sqs[:], xr, sbc[:]),
                               waits=[t_f, t_sbc, t_nmbc] + prev_y)
                    t_y2 = vop(lambda e: e.tensor_add(sgs[:], sqs[:], nmbc[:]),
                               waits=[t_y1])
                    t_y3 = vop(lambda e: e.tensor_add(yt, sgs[:], hidf[:]),
                               waits=[t_y2, t_x2] + prev_y)
                    t_out = dma("gpsimd", "sO",
                                y_fm[j * P:(j + 1) * P, fc * TB:(fc + 1) * TB],
                                yt, waits=[t_y3])
                    fr_busy[0] = [t_y1]
                    fr_busy[1] = [t_y1]
                    hid_b = [t_y3]
                    prev_y = [t_y3, t_out]

            n_out = em.cur("sO")
            em.emit("gpsimd", lambda e: e.nop(), waits=[("sO", n_out)])

        with ExitStack() as sem_ctx:
            sems = {s: sem_ctx.enter_context(nc.semaphore(s)) for s in SEM_NAMES}
            block = sem_ctx.enter_context(nc.Block())

            def run(eng_name, eng):
                for waits, fn, inc in em.ops[eng_name]:
                    for s, v in waits:
                        eng.wait_ge(sems[s], v)
                    inst = fn(eng)
                    if inc is not None:
                        s, amt = inc
                        inst.then_inc(sems[s], amt)

            @block.sync
            def _(sync):
                run("sync", sync)

            @block.tensor
            def _(tensor):
                run("tensor", tensor)

            @block.vector
            def _(vector):
                run("vector", vector)

            @block.scalar
            def _(scalar):
                run("scalar", scalar)

            @block.gpsimd
            def _(gpsimd):
                run("gpsimd", gpsimd)

    return nc


_NC_CACHE = None


def _get_nc():
    global _NC_CACHE
    if _NC_CACHE is None:
        _NC_CACHE = build_nc()
    return _NC_CACHE


def _tile_lhsT(w, nk, nm):
    t = w.T.reshape(nk, P, nm, P).transpose(2, 1, 0, 3).reshape(nm * P, nk * P)
    return np.ascontiguousarray(t)


def kernel(L0_attn_hidden_states, L0_attn_Wi, L0_attn_Wf, L0_attn_Wg,
           L0_attn_Wo, L0_attn_gnorm_w, L0_ffn_W1, L0_ffn_W2, L0_ffn_W3):
    import ml_dtypes
    x = np.ascontiguousarray(np.asarray(L0_attn_hidden_states, dtype=np.float32))
    wi = _tile_lhsT(np.asarray(L0_attn_Wi, dtype=np.float32), NK, NT)
    wf = _tile_lhsT(np.asarray(L0_attn_Wf, dtype=np.float32), NK, NT)
    wg = _tile_lhsT(np.asarray(L0_attn_Wg, dtype=np.float32), NK, NT)
    wo = _tile_lhsT(np.asarray(L0_attn_Wo, dtype=np.float32), NK, NT)
    w1 = _tile_lhsT(np.asarray(L0_ffn_W1, dtype=np.float32), NK, NDH)
    w3 = _tile_lhsT(np.asarray(L0_ffn_W3, dtype=np.float32), NK, NDH)
    w2 = np.ascontiguousarray(
        np.asarray(L0_ffn_W2, dtype=np.float32).T.reshape(NDH, P, NG, TB)
        .transpose(2, 0, 1, 3).reshape(NG * NDH * P, TB))
    gnw = np.ascontiguousarray(
        np.asarray(L0_attn_gnorm_w, dtype=np.float32).reshape(NT, P).T)
    i = np.arange(P, dtype=np.uint32)
    anded = i[:, None] & i[None, :]
    pop = np.zeros((P, P), dtype=np.int64)
    for b in range(7):
        pop += (anded >> b) & 1
    h128 = np.where(pop % 2 == 0, 1.0, -1.0).astype(ml_dtypes.bfloat16)
    onescol = np.ones((P, 1), dtype=ml_dtypes.bfloat16)
    onesrow = np.ones((1, P), dtype=np.float32)

    nc = _get_nc()
    in_maps = []
    for c in range(8):
        b, half = c // 2, c % 2
        xm = np.ascontiguousarray(x[b, half * TCORE:(half + 1) * TCORE, :].T)
        xc = np.ascontiguousarray(x[b, 0:TCORE, :].T)
        mask = np.full((P, 1), float(half), dtype=np.float32)
        in_maps.append({
            "x_fm": xm, "xc_fm": xc, "wi": wi, "wf": wf, "wg": wg, "wo": wo,
            "w1": w1, "w3": w3, "w2": w2, "gnw": gnw, "h128": h128,
            "onescol": onescol, "onesrow": onesrow, "maskcol": mask,
        })
    res = run_bass_kernel_spmd(nc, in_maps, core_ids=list(range(8)))
    out = np.empty_like(x)
    for c in range(8):
        b, half = c // 2, c % 2
        out[b, half * TCORE:(half + 1) * TCORE, :] = res.results[c]["y_fm"].T
    return out
